# version 05 (final)
# baseline (speedup 1.0000x reference)
import math
import numpy as np
import ml_dtypes
import concourse.bass as bass
import concourse.mybir as mybir
from concourse.bass_utils import run_bass_kernel_spmd

F32 = mybir.dt.float32
BF16 = mybir.dt.bfloat16
I32 = mybir.dt.int32
AF = mybir.ActivationFunctionType
ALU = mybir.AluOpType
AXX = mybir.AxisListType.X
BF = ml_dtypes.bfloat16

B = 2
SEQ = 4096
DM = 2048
DI = 4096
DS = 16
DTR = 128
TOK = 1024
HALO = 3
H = 32
HD = 64
BS = 256
NBLK = 4
F0 = 8192
EPS_LN = 1e-5
EPS_Q = 1e-5
EPS_RMS1 = 1e-5
EPS_RMS2 = float(np.finfo(np.float32).eps)
QTH = 0.5 + 2.0 ** -10

_base = 1.0 / 2.0 ** (8.0 / H)
_factor = 1.0 - 0.0 / (32 - 1 + 1e-05) + 1e-05
SLOPES = [float((_base ** (h + 1)) * _factor) for h in range(H)]
BD = [float(math.exp(-s * BS)) for s in SLOPES]
BD4 = [float(b ** NBLK) for b in BD]

QNAME = {"t": "tensor", "v": "vector", "s": "scalar", "g": "gpsimd", "y": "sync"}


class KB:

    def __init__(self, nc):
        self.nc = nc
        self.steps = []
        self.cnt = {}

    def _tok(self, sem):
        self.cnt[sem] = self.cnt.get(sem, 0) + 1
        return (sem, self.cnt[sem])

    def op(self, q, fn, after=()):
        tok = self._tok(q)
        self.steps.append((q, fn, tuple(a for a in after if a is not None), q))
        return tok

    def dma(self, q, out, in_, after=(), sem=None, slow=False):
        sem = (q + "d") if sem is None else sem
        tok = self._tok(sem)
        eng = getattr(self.nc, QNAME[q])
        fn = (lambda eng=eng, out=out, in_=in_, slow=slow:
              eng.dma_start(out=out, in_=in_, allow_slow_non_contiguous=slow))
        self.steps.append((q, fn, tuple(a for a in after if a is not None), sem))
        return tok

    def fence(self, q, after):
        self.steps.append((q, None, tuple(a for a in after if a is not None), q))

    def build(self):
        nc = self.nc
        sem_names = sorted(set(list(self.cnt.keys()) + ["t", "v", "s", "g"]))
        ctxs = []
        sems = {}
        for s in sem_names:
            c = nc.semaphore("sem_" + s)
            sems[s] = c.__enter__()
            ctxs.append(c)
        blk = nc.Block()
        block = blk.__enter__()
        ctxs.append(blk)

        def make_runner(qn):
            def run(section):
                hwm = {}
                for (sq, fn, waits, sem) in self.steps:
                    if sq != qn:
                        continue
                    for (ws, wc) in waits:
                        if hwm.get(ws, 0) >= wc:
                            continue
                        section.wait_ge(sems[ws], wc * (16 if ws.endswith("d") else 1))
                        hwm[ws] = wc
                    if fn is None:
                        continue
                    fn().then_inc(sems[sem], 16 if sem.endswith("d") else 1)
            return run

        @block.tensor
        def _(tensor):
            make_runner("t")(tensor)

        @block.vector
        def _(vector):
            make_runner("v")(vector)

        @block.scalar
        def _(scalar):
            make_runner("s")(scalar)

        @block.gpsimd
        def _(gpsimd):
            make_runner("g")(gpsimd)

        @block.sync
        def _(sync):
            make_runner("y")(sync)

        for c in reversed(ctxs):
            c.__exit__(None, None, None)


class Prog:
    def __init__(self):
        self.nc = bass.Bass("TRN2", target_bir_lowering=False)
        self.kb = KB(self.nc)
        self._ctx = []
        nc, kb = self.nc, self.kb

        def mm(ps_ap, lhsT, rhs, start, stop, after):
            return kb.op("t", lambda: nc.tensor.matmul(ps_ap, lhsT, rhs, start=start, stop=stop), after)

        def tr(ps_ap, in_ap, ident, after):
            return kb.op("t", lambda: nc.tensor.transpose(ps_ap, in_ap, ident), after)

        def vo(q, meth, after, **kw):
            eng = getattr(nc, QNAME[q])
            return kb.op(q, lambda m=meth, e=eng, k=kw: getattr(e, m)(**k), after)

        def ac(after, **kw):
            extra = []
            for key in ("scale", "bias"):
                v = kw.get(key, None)
                if isinstance(v, (int, float)) and float(v) not in (0.0, 1.0):
                    extra.append(self.constf(float(v)))
            return kb.op("s", lambda k=kw: nc.scalar.activation(**k), after + type(after)(extra) if isinstance(after, tuple) else list(after) + extra)

        self.mm, self.tr, self.vo, self.ac = mm, tr, vo, ac
        self.dummy = None
        self._consts = {}
        self._cpool = None

    def constf(self, val):
        val = float(val)
        if val in self._consts:
            return self._consts[val]
        if self._cpool is None:
            self._cpool = self.sb("constpool", [128, 88])
        idx = len(self._consts)
        ap = self._cpool[:, idx:idx + 1]
        tok = self.vo("v", "memset", [], ap=ap, constant=val)
        self.nc.const_aps.register(F32, val, ap)
        self._consts[val] = tok
        return tok

    def inp(self, name, shape, dt=F32):
        return self.nc.dram_tensor(name, list(shape), dt, kind="ExternalInput").ap()

    def outp(self, name, shape, dt=F32):
        return self.nc.dram_tensor(name, list(shape), dt, kind="ExternalOutput").ap()

    def scratch(self, name, shape, dt=F32):
        return self.nc.dram_tensor(name, list(shape), dt, kind="Internal").ap()

    def sb(self, name, shape, dt=F32):
        t = self.nc.sbuf_tensor(name, list(shape), dt)
        h = t.__enter__()
        self._ctx.append(t)
        return h

    def ps(self, name, shape, dt=F32):
        t = self.nc.psum_tensor(name, list(shape), dt)
        h = t.__enter__()
        self._ctx.append(t)
        return h

    def mark(self, after):
        if self.dummy is None:
            self.dummy = self.sb("dummy_bar", [1, 64])
        self._mrkn = getattr(self, "_mrkn", 0) + 1
        i = self._mrkn % 64
        return self.vo("v", "memset", after, ap=self.dummy[0:1, i:i + 1], constant=0.0)

    def barrier(self):
        waits = [(s, c) for s, c in self.kb.cnt.items()]
        return self.mark(waits)

    def close(self):
        self.kb.build()
        for c in reversed(self._ctx):
            c.__exit__(None, None, None)


def sb_tiles(wT, mt=128):
    K, M = wT.shape
    return np.ascontiguousarray(wT.reshape(K // 128, 128, M // mt, mt).transpose(2, 1, 0, 3))


def col_tiles(v, p=128):
    return np.ascontiguousarray(np.asarray(v).reshape(-1, p).T)


def ptile3(a, p=128):
    R, C = a.shape
    return np.ascontiguousarray(a.reshape(R // p, p, C).transpose(1, 0, 2))



def emit_round(P, y_ap, out_bf_ap, ti_ap, e_ap, r1_ap, after):
    vo = P.vo
    t1 = vo("v", "tensor_copy", after, out=ti_ap, in_=y_ap)
    t2 = vo("v", "tensor_tensor", [t1], out=e_ap, in0=y_ap, in1=ti_ap, op=ALU.subtract)
    t3 = vo("v", "scalar_tensor_tensor", [t2], out=r1_ap, in0=e_ap, scalar=0.5,
            in1=ti_ap, op0=ALU.is_ge, op1=ALU.add)
    t4 = vo("v", "scalar_tensor_tensor", [t3], out=e_ap, in0=e_ap, scalar=-0.5,
            in1=r1_ap, op0=ALU.is_lt, op1=ALU.subtract)
    return vo("v", "tensor_scalar_mul", [t4], out=out_bf_ap, in0=e_ap, scalar1=-1.0)


def emit_tern_group(P, w_ap, wb_ap, c2_ap, tp_ap, tn_ap, ts_ap, hat_ap,
                    sc_ap, scb_ap, th_ap, tnh_ap, ng, after):
    vo = P.vo
    t_wb = vo("v", "tensor_copy", after, out=wb_ap, in_=w_ap)
    t_c2 = vo("g", "tensor_copy", [t_wb], out=c2_ap, in_=wb_ap)
    t_ab = vo("v", "scalar_tensor_tensor", [t_c2], out=tp_ap, in0=c2_ap, scalar=-1.0,
              in1=c2_ap, op0=ALU.mult, op1=ALU.max)
    t_rs = vo("v", "tensor_reduce", [t_ab], out=sc_ap[:, 0:ng],
              in_=tp_ap.rearrange("p (g x) -> p g x", x=64), axis=AXX, op=ALU.add)
    t_sc = vo("v", "tensor_scalar_mul", [t_rs], out=sc_ap[:, 0:ng], in0=sc_ap[:, 0:ng],
              scalar1=1.0 / 64.0)
    t_scb = vo("v", "tensor_copy", [t_sc], out=scb_ap[:, 0:ng], in_=sc_ap[:, 0:ng])
    t_sc2 = vo("v", "tensor_copy", [t_scb], out=sc_ap[:, 0:ng], in_=scb_ap[:, 0:ng])
    t_sc3 = vo("v", "tensor_scalar_max", [t_sc2], out=sc_ap[:, 0:ng], in0=sc_ap[:, 0:ng],
               scalar1=1e-8)
    t_th = vo("v", "tensor_scalar_mul", [t_sc3], out=th_ap[:, 0:ng], in0=sc_ap[:, 0:ng],
              scalar1=QTH)
    t_tnh = vo("v", "tensor_scalar_mul", [t_th], out=tnh_ap[:, 0:ng], in0=th_ap[:, 0:ng],
               scalar1=-1.0)
    thb = th_ap[:, 0:ng].unsqueeze(2).broadcast_to((128, ng, 64))
    tnb = tnh_ap[:, 0:ng].unsqueeze(2).broadcast_to((128, ng, 64))
    c2v = c2_ap.rearrange("p (g x) -> p g x", x=64)
    t_p = vo("v", "tensor_tensor", [t_th], out=tp_ap.rearrange("p (g x) -> p g x", x=64),
             in0=c2v, in1=thb, op=ALU.is_gt)
    t_n = vo("v", "tensor_tensor", [t_tnh], out=tn_ap.rearrange("p (g x) -> p g x", x=64),
             in0=c2v, in1=tnb, op=ALU.is_lt)
    t_t = vo("v", "tensor_tensor", [t_p, t_n], out=ts_ap, in0=tp_ap, in1=tn_ap,
             op=ALU.subtract)
    scbb = scb_ap[:, 0:ng].unsqueeze(2).broadcast_to((128, ng, 64))
    return vo("v", "tensor_tensor", [t_t],
              out=hat_ap.rearrange("p (g x) -> p g x", x=64),
              in0=ts_ap.rearrange("p (g x) -> p g x", x=64), in1=scbb, op=ALU.mult)


def build_A():
    P = Prog()
    kb = P.kb
    mm, tr, vo, ac = P.mm, P.tr, P.vo, P.ac

    u9 = P.inp("u9", [9, 128, DM])
    cvec = P.inp("cvec", [128, 4, 16])
    wm0 = P.inp("wm0", [16, 128, 1536])
    wm1 = P.inp("wm1", [16, 128, 1536])
    wmsa = P.inp("wmsa", [16, 128, 4096])
    w_z = P.inp("w_z", [32, 128, 16, 128])
    w_x = P.inp("w_x", [32, 128, 16, 128])
    b_z = P.inp("b_z", [128, 32])
    b_x = P.inp("b_x", [128, 32])
    convw = P.inp("convw", [128, 32, 4])
    convb = P.inp("convb", [128, 32])
    dpar = P.inp("dpar", [128, 32])
    alog = P.inp("alog", [128, 32, 16])
    bdt_i = P.inp("bdt", [128, 32])
    wxdt = P.inp("wxdt", [128, 32, 128])
    wxbc = P.inp("wxbc", [128, 32, 32])
    wdt_i = P.inp("wdt", [128, 32, 128])
    mask3 = P.inp("mask3", [128, 3])
    idn = P.inp("idn", [128, 128])
    selbc = P.inp("selbc", [32, 4096])
    w1gu_q = P.inp("w1gu_q", [16, 128, DM])
    w1pj_q = P.inp("w1pj_q", [2, 128, F0])
    w0gu_a = P.inp("w0gu_a", [16, 128, DM])
    w0dn_a = P.inp("w0dn_a", [2, 128, F0])

    modsh = P.outp("modsh", [2, 2, 1536])
    zsil = P.outp("zsil", [128, 32, TOK], BF16)
    yloc = P.outp("yloc", [128, 32, TOK])
    cumdt_o = P.outp("cumdt", [128, 32, TOK])
    cmt_o = P.outp("cmt", [16, TOK])
    svec = P.outp("svec", [128, 32, 16])
    pvec = P.outp("pvec", [128, 32, 16])
    w0part = P.outp("w0part", [128, 2])
    w1gu_h = P.outp("w1gu_h", [16, 128, DM], BF16)
    w1pj_h = P.outp("w1pj_h", [2, 128, F0], BF16)
    xs_d = P.scratch("xs_d", [128, 32, TOK])

    TA = TOK + HALO

    AR = P.sb("AR", [128, 16384])
    fl = AR.ap()

    def arv(a, b):
        return fl[:, a:b]

    id_s = P.sb("id_s", [128, 128])
    cv_s = P.sb("cv_s", [128, 4, 16])
    cv_sil = P.sb("cv_sil", [128, 4, 16])
    msh_s = P.sb("msh_s", [2, 3072])
    saca = P.sb("saca", [1, 4096])
    ones1 = P.sb("ones1", [1, 128])
    colp = P.sb("colp", [128, 8])
    h1T = P.sb("h1T", [128, 16, 1030], BF16)
    BFA = P.sb("BFA", [128, 6144], BF16)
    bfa = BFA.ap()
    wstrb = [bfa[:, 0:2048].rearrange("p (a b) -> p a b", a=16),
             bfa[:, 2048:4096].rearrange("p (a b) -> p a b", a=16)]
    bz_c = P.sb("bz_c", [128, 32])
    bx_c = P.sb("bx_c", [128, 32])
    cw_c = P.sb("cw_c", [128, 32, 4])
    cb_c = P.sb("cb_c", [128, 32])
    dp_c = P.sb("dp_c", [128, 32])
    al_s = P.sb("al_s", [128, 32, 16])
    at_s = P.sb("at_s", [128, 32, 16])
    bdt_c = P.sb("bdt_c", [128, 32])
    m3_s = P.sb("m3_s", [128, 3])
    xst = [P.sb(f"xst{i}", [128, TOK]) for i in range(2)]
    xsb = [bfa[:, 4096:5120], bfa[:, 5120:6144]]
    zsb = bfa[:, 4096:5120]
    wxb_b = P.sb("wxb_b", [128, 32, 128], BF16)
    wbc_b = P.sb("wbc_b", [128, 32, 32], BF16)
    wdt_b = P.sb("wdt_b", [128, 32, 128], BF16)
    lowr = P.sb("lowr", [128, TOK])
    lowrb = P.sb("lowrb", [128, TOK], BF16)
    bc_s = P.sb("bc_s", [32, TOK])
    brep = P.sb("brep", [128, 512])
    crep = P.sb("crep", [128, 512])
    sful = P.sb("sful", [128, 32, 16])
    pful = P.sb("pful", [128, 32, 16])
    hstc = P.sb("hstc", [128, 8, 16])
    ccol = P.sb("ccol", [128, 8])
    zcol = P.sb("zcol", [128, 1])
    ones_t = P.sb("ones_t", [128, 512])
    sel_s = P.sb("sel_s", [32, 4096])
    q_sc = P.sb("q_sc", [128, 32])
    q_scb = P.sb("q_scb", [128, 32], BF16)
    q_th = P.sb("q_th", [128, 32])
    q_tnh = P.sb("q_tnh", [128, 32])
    pacc = P.sb("pacc", [128, 2])

    wstr = [arv(0, 2048).rearrange("p (a b) -> p a b", a=16),
            arv(2048, 4096).rearrange("p (a b) -> p a b", a=16)]
    u_s = arv(0, 2048)
    ln_a = arv(2048, 4096)
    ln_sq = arv(4096, 6144)
    sa_rep = arv(6144, 8192)
    ca1_rep = arv(8192, 10240)
    xh = [arv(10240, 10240 + TA), arv(11270, 11270 + TA)]
    cva = arv(12300, 13324)
    cvb = arv(13324, 14348)
    dtg = arv(0, 4096).rearrange("p (d t) -> p d t", d=8)
    dxg = arv(4096, 8192).rearrange("p (d t) -> p d t", d=8)
    ygv = arv(8192, 12288).rearrange("p (d t) -> p d t", d=8)
    dA_t = [arv(12288, 12800), arv(12800, 13312)]
    du_t = [arv(13312, 13824), arv(13824, 14336)]
    hh_t = [arv(14336, 14848), arv(14848, 15360)]
    ct_t = [arv(15360, 15872), arv(15872, 16384)]
    q_w = [arv(0, 2048), arv(2048, 4096)]
    q_c2 = arv(4096, 6144)
    q_tp = arv(6144, 8192)
    q_tn = arv(8192, 10240)
    p_w = [arv(0, 8192), arv(8192, 16384)]
    q_wb = bfa[:, 0:2048]
    q_ts = bfa[:, 2048:4096]
    q_hat = bfa[:, 4096:6144]

    ps = [P.ps(f"ps{i}", [128, 512]) for i in range(6)]
    ps6 = P.ps("ps6", [32, 512])
    ps7 = P.ps("ps7", [32, 512])

    kb.dma("y", id_s[:], idn)
    kb.dma("y", sel_s[:], selbc)
    kb.dma("y", cv_s[:], cvec)
    for sb_, dr in [(bz_c, b_z), (bx_c, b_x), (cw_c, convw), (cb_c, convb),
                    (dp_c, dpar), (al_s, alog), (bdt_c, bdt_i), (m3_s, mask3)]:
        kb.dma("y", sb_[:], dr)
    t_consts = P.barrier()
    t_cvs = ac([t_consts], out=cv_sil[:], in_=cv_s[:], func=AF.Silu)
    vo("v", "memset", [], ap=ones1[:], constant=1.0)
    vo("v", "memset", [], ap=ones_t[:], constant=1.0)
    vo("v", "memset", [], ap=zcol[:], constant=0.0)
    t_at = ac([t_consts], out=at_s[:], in_=al_s[:], func=AF.Exp)
    t_at = vo("v", "tensor_scalar_mul", [t_at], out=at_s[:], in0=at_s[:], scalar1=-1.0)
    st1 = arv(0, 4096).rearrange("p (a b) -> p a b", a=32)
    st2 = arv(4096, 8192).rearrange("p (a b) -> p a b", a=32)
    st3 = arv(8192, 9216).rearrange("p (a b) -> p a b", a=32)
    kb.dma("y", st1, wxdt, after=[t_consts])
    kb.dma("y", st2, wdt_i, after=[t_consts])
    d3 = kb.dma("y", st3, wxbc, after=[t_consts])
    vo("v", "tensor_copy", [d3], out=wxb_b[:], in_=st1)
    vo("v", "tensor_copy", [d3], out=wdt_b[:], in_=st2)
    vo("v", "tensor_copy", [d3], out=wbc_b[:], in_=st3)

    t_p1 = P.barrier()
    wl1 = [None, None]
    pev1 = None
    for li, wm in enumerate((wm0, wm1)):
        for ch in range(3):
            toks = []
            for k in range(16):
                p2 = k % 2
                buf = wstr[p2][:, 0:4, :].rearrange("p a b -> p (a b)")
                dl = kb.dma("y", buf, wm[k, :, 512 * ch:512 * ch + 512], sem=f"w{p2}d",
                            after=[t_p1, wl1[p2]])
                wl1[p2] = mm(ps[5][0:2, :], cv_sil[:, 2 * li:2 * li + 2, k], buf, k == 0, k == 15,
                             [dl, t_cvs, t_p1, pev1])
                toks.append(wl1[p2])
            o0 = 1536 * li + 512 * ch
            ev = vo("v", "tensor_copy", [toks[-1]], out=msh_s[:, o0:o0 + 512],
                    in_=ps[5][0:2, :])
            pev1 = ev
            kb.dma("g", modsh[li, :, 512 * ch:512 * ch + 512], msh_s[:, o0:o0 + 512],
                   after=[ev])
    t_saca = []
    wl2 = [None, None]
    for ch in range(8):
        toks = []
        for k in range(16):
            p2 = k % 2
            buf = wstr[p2][:, 4:8, :].rearrange("p a b -> p (a b)")
            dl = kb.dma("s", buf, wmsa[k, :, 512 * ch:512 * ch + 512], sem=f"sa{p2}d",
                        after=[t_p1, wl2[p2]])
            wl2[p2] = mm(ps[4][0:1, :], cv_sil[:, 0, k:k + 1], buf, k == 0, k == 15,
                         [dl, t_cvs, t_p1, (t_saca[-1] if t_saca else None)])
            toks.append(wl2[p2])
        t_saca.append(vo("v", "tensor_copy", [toks[-1]],
                         out=saca[0:1, 512 * ch:512 * ch + 512], in_=ps[4][0:1, :]))
    prev0, prev1 = None, None
    for half in range(4):
        sl = slice(512 * half, 512 * half + 512)
        tm = mm(ps[0][:, :], ones1[0:1, :], saca[0:1, sl], True, True,
                [t_saca[half], prev0])
        prev0 = vo("v", "tensor_copy", [tm], out=sa_rep[:, sl], in_=ps[0][:, :])
        tm2 = mm(ps[1][:, :], ones1[0:1, :],
                 saca[0:1, 2048 + 512 * half:2048 + 512 * half + 512], True, True,
                 [t_saca[4 + half], prev1])
        prev1 = vo("v", "tensor_scalar_add", [tm2], out=ca1_rep[:, sl], in0=ps[1][:, :],
                   scalar1=1.0)

    t_p2 = P.barrier()
    prev_tr = None
    prev_last = None
    for tt8 in range(9):
        dl = kb.dma("y", u_s, u9[tt8], sem="u0d",
                    after=[t_p2] + ([prev_last] if prev_last else []))
        r1 = vo("v", "tensor_reduce", [dl], out=colp[:, 0:1], in_=u_s, axis=AXX, op=ALU.add)
        r1 = vo("v", "tensor_scalar_mul", [r1], out=colp[:, 0:1], in0=colp[:, 0:1],
                scalar1=1.0 / DM)
        xc = vo("v", "tensor_scalar", [r1], out=ln_a, in0=u_s, scalar1=colp[:, 0:1],
                scalar2=None, op0=ALU.subtract)
        sq = ac([xc], out=ln_sq, in_=ln_a, func=AF.Square)
        r2 = vo("v", "tensor_reduce", [sq], out=colp[:, 1:2], in_=ln_sq, axis=AXX, op=ALU.add)
        sd = ac([r2], out=colp[:, 2:3], in_=colp[:, 1:2], func=AF.Sqrt, scale=1.0 / DM,
                bias=EPS_LN)
        rc = vo("v", "reciprocal", [sd], out=colp[:, 3:4], in_=colp[:, 2:3])
        xn = vo("v", "tensor_scalar", [rc], out=ln_a, in0=ln_a, scalar1=colp[:, 3:4],
                scalar2=None, op0=ALU.mult)
        h1 = vo("v", "tensor_tensor", [xn], out=ln_sq, in0=ln_a, in1=ca1_rep, op=ALU.mult)
        h1 = vo("v", "tensor_tensor", [h1], out=ln_sq, in0=ln_sq, in1=sa_rep, op=ALU.add)
        c0 = 128 * tt8
        w = min(1030 - c0, 128)
        for kf in range(16):
            tk = tr(ps[4][:, 0:128], ln_sq[:, 128 * kf:128 * kf + 128], id_s[:],
                    [h1, prev_tr, t_consts])
            prev_tr = vo("v", "tensor_copy", [tk], out=h1T[:, kf, c0:c0 + w],
                         in_=ps[4][:, 0:w])
        prev_last = prev_tr

    t_p3 = P.barrier()
    zt = []
    wlast = [None, None]
    prev_evs = []
    for m in range(32):
        b0 = m % 2
        dl = kb.dma("y", wstr[b0].rearrange("p a b -> p (a b)"),
                    w_z[m].rearrange("p a b -> p (a b)"), sem=f"w{b0}d",
                    after=[t_p3, wlast[b0]])
        cb = vo("g", "tensor_copy", [dl, wlast[b0]], out=wstrb[b0], in_=wstr[b0])
        evs = []
        for ch in range(2):
            toks = []
            for k in range(16):
                toks.append(mm(ps[ch][:, :], wstrb[b0][:, k, :],
                               h1T[:, k, 3 + 512 * ch:3 + 512 * ch + 512],
                               k == 0, k == 15,
                               [cb, t_p3] + prev_evs))
            evs.append(ac([toks[-1]] + ([zt[-1]] if zt else []),
                          out=zsb[:, 512 * ch:512 * ch + 512], in_=ps[ch][:, :],
                          func=AF.Silu, bias=bz_c[:, m:m + 1]))
        wlast[b0] = toks[-1]
        prev_evs = evs
        zt.append(kb.dma("g", zsil[:, m, :], zsb, sem="zsd", after=evs))

    t_p4 = P.barrier()
    lr_toks = []
    xdma = [None, None]
    prev_ev = {}
    prev_sl = [None, None]
    prev_sl_any = None
    wlast = [None, None]
    lr_last = [None, None]
    for m in range(32):
        b0 = m % 2
        dl = kb.dma("y", wstr[b0].rearrange("p a b -> p (a b)"),
                    w_x[m].rearrange("p a b -> p (a b)"), sem=f"w{b0}d",
                    after=[t_p4, wlast[b0]])
        cb = vo("g", "tensor_copy", [dl, wlast[b0]], out=wstrb[b0], in_=wstrb[b0] if False else wstr[b0])
        evs = []
        for ch, (c0, w) in enumerate(((0, 512), (512, 512), (1024, 3))):
            pp = ps[ch] if ch < 2 else ps[5]
            toks = []
            for k in range(16):
                toks.append(mm(pp[:, 0:w], wstrb[b0][:, k, :], h1T[:, k, c0:c0 + w],
                               k == 0, k == 15,
                               [cb, t_p4] + ([prev_ev[ch]] if ch in prev_ev else [])))
            ev = vo("v", "tensor_scalar", [toks[-1], prev_sl[b0]], out=xh[b0][:, c0:c0 + w],
                    in0=pp[:, 0:w], scalar1=bx_c[:, m:m + 1], scalar2=None, op0=ALU.add)
            prev_ev[ch] = ev
            evs.append(ev)
        wlast[b0] = toks[-1]
        mk = vo("v", "tensor_tensor", evs, out=xh[b0][:, 0:3], in0=xh[b0][:, 0:3],
                in1=m3_s[:], op=ALU.mult)
        t = vo("v", "tensor_scalar", [mk, prev_sl_any], out=cva, in0=xh[b0][:, 0:TOK],
               scalar1=cw_c[:, m, 0:1], scalar2=None, op0=ALU.mult)
        src, dst = cva, cvb
        for kk in range(1, 4):
            tk1 = vo("g", "tensor_scalar", [t], out=dst, in0=xh[b0][:, kk:kk + TOK],
                     scalar1=cw_c[:, m, kk:kk + 1], scalar2=None, op0=ALU.mult)
            t = vo("v", "tensor_tensor", [tk1], out=dst, in0=dst, in1=src, op=ALU.add)
            src, dst = dst, src
        xi = m % 2
        sl = ac([t, xdma[xi]], out=xst[xi][:], in_=src, func=AF.Silu,
                bias=cb_c[:, m:m + 1])
        prev_sl[b0] = sl
        prev_sl_any = sl
        xb = vo("v", "tensor_copy", [sl, lr_last[xi]], out=xsb[xi], in_=xst[xi][:])
        xdma[xi] = kb.dma("g", xs_d[:, m, :], xst[xi][:], sem=f"xd{xi}d", after=[sl])
        for ch in range(2):
            lr_toks.append(mm(ps[2 + ch][:, :], wxb_b[:, m, :],
                              xsb[xi][:, 512 * ch:512 * ch + 512], m == 0, m == 31, [xb]))
            lr_toks.append(mm((ps6 if ch == 0 else ps7)[:, :], wbc_b[:, m, :],
                              xsb[xi][:, 512 * ch:512 * ch + 512], m == 0, m == 31, [xb]))
        lr_last[xi] = lr_toks[-1]
    ev_lr = [vo("v", "tensor_copy", [lr_toks[-4]], out=lowr[:, 0:512], in_=ps[2][:, :]),
             vo("v", "tensor_copy", [lr_toks[-2]], out=lowr[:, 512:1024], in_=ps[3][:, :]),
             vo("v", "tensor_copy", [lr_toks[-3]], out=bc_s[:, 0:512], in_=ps6[:, :]),
             vo("v", "tensor_copy", [lr_toks[-1]], out=bc_s[:, 512:1024], in_=ps7[:, :])]
    t_lrb = vo("v", "tensor_copy", [ev_lr[0], ev_lr[1]], out=lowrb[:], in_=lowr[:])
    kb.dma("g", cmt_o, bc_s[16:32, :], after=[ev_lr[2], ev_lr[3]])

    t_p5 = P.barrier()
    yl_last = None
    for g in range(4):
        for th in range(2):
            hw = 512 * th
            dt_done = []
            for d in range(8):
                m = 8 * g + d
                xi = d % 2
                dx = kb.dma("y", xst[xi][:, 0:512], xs_d[:, m, hw:hw + 512], sem=f"x{xi}d",
                            after=[t_p5, yl_last] +
                                  ([dt_done[-2]] if len(dt_done) >= 2 else []))
                mm1 = mm(ps[d % 2][:, :], wdt_b[:, m, :], lowrb[:, hw:hw + 512],
                         True, True, [t_lrb, t_p5, dx])
                e1 = ac([mm1], out=dtg[:, d, :], in_=ps[d % 2][:, :], func=AF.Exp,
                        bias=bdt_c[:, m:m + 1])
                e2 = ac([e1], out=dtg[:, d, :], in_=dtg[:, d, :], func=AF.Ln, bias=1.0)
                ini = zcol[:, 0:1] if th == 0 else ccol[:, d:d + 1]
                cs = vo("v", "tensor_tensor_scan", [e2], out=ct_t[d % 2], data0=ones_t[:],
                        data1=dtg[:, d, :], initial=ini, op0=ALU.mult, op1=ALU.add)
                cc = vo("v", "tensor_copy", [cs], out=ccol[:, d:d + 1],
                        in_=ct_t[d % 2][:, 511:512])
                cd = kb.dma("g", cumdt_o[:, m, hw:hw + 512], ct_t[d % 2], sem="cdd",
                            after=[cc])
                if th == 1:
                    ac([cc], out=pful[:, m, :], in_=at_s[:, m, :], func=AF.Exp,
                       scale=ccol[:, d:d + 1])
                dxk = vo("v", "tensor_tensor", [dx, e2], out=dxg[:, d, :], in0=dtg[:, d, :],
                         in1=xst[xi][:, 0:512], op=ALU.mult)
                y0 = vo("g", "tensor_scalar", [dx, dxk], out=ygv[:, d, :],
                        in0=xst[xi][:, 0:512], scalar1=dp_c[:, m:m + 1], scalar2=None,
                        op0=ALU.mult)
                dt_done.append(P.mark([dxk, y0, cd]))
            t_bar = dt_done[-1]
            prev_n = [t_bar]
            prev_hh = [None, None]
            prev_yt = [None, None]
            for n in range(16):
                tb = mm(ps[2][:, :], sel_s[:, 128 * n:128 * n + 128],
                        bc_s[:, hw:hw + 512], True, True, [t_bar, t_consts] + prev_n[-2:])
                tbc = vo("v", "tensor_copy", [tb], out=brep[:], in_=ps[2][:, :])
                tc = mm(ps[3][:, :], sel_s[:, 128 * (16 + n):128 * (16 + n) + 128],
                        bc_s[:, hw:hw + 512], True, True, [t_bar, t_consts] + prev_n[-2:])
                tcc = vo("v", "tensor_copy", [tc], out=crep[:], in_=ps[3][:, :])
                prev_n = []
                for d in range(8):
                    m = 8 * g + d
                    i2 = d % 2
                    da = ac([t_bar, prev_hh[i2]], out=dA_t[i2], in_=dtg[:, d, :],
                            func=AF.Exp, scale=at_s[:, m, n:n + 1])
                    du = vo("v", "tensor_tensor", [da, tbc, prev_yt[i2]], out=du_t[i2],
                            in0=dxg[:, d, :], in1=brep[:], op=ALU.mult)
                    ini = zcol[:, 0:1] if th == 0 else hstc[:, d, n:n + 1]
                    hh = vo("v", "tensor_tensor_scan", [du, prev_yt[i2]], out=hh_t[i2],
                            data0=dA_t[i2], data1=du_t[i2], initial=ini,
                            op0=ALU.mult, op1=ALU.add)
                    prev_hh[i2] = hh
                    hc = vo("v", "tensor_copy", [hh],
                            out=(hstc[:, d, n:n + 1] if th == 0 else sful[:, m, n:n + 1]),
                            in_=hh_t[i2][:, 511:512])
                    yt = vo("g", "tensor_tensor", [hh, tcc], out=du_t[i2], in0=hh_t[i2],
                            in1=crep[:], op=ALU.mult)
                    prev_yt[i2] = yt
                    ya = vo("v", "tensor_tensor", [yt], out=ygv[:, d, :], in0=ygv[:, d, :],
                            in1=du_t[i2], op=ALU.add)
                    prev_n.append(ya)
                    prev_n.append(hc)
            for d in range(8):
                m = 8 * g + d
                yl_last = kb.dma("g", yloc[:, m, hw:hw + 512], ygv[:, d, :], sem="yld",
                                 after=prev_n[-2:])
    t_send = P.barrier()
    t_sv = kb.dma("y", svec, sful[:], after=[t_send])
    t_pv = kb.dma("y", pvec, pful[:], after=[t_send])

    pz = vo("v", "memset", [t_send], ap=pacc[:], constant=0.0)
    t_pp = [pz]
    ldp = [None, None]
    for which, (src, nt) in enumerate(((w0gu_a, 16), (w0dn_a, 2))):
        for it in range(nt):
            pty = it % 2
            buf = (q_w[pty] if which == 0 else p_w[pty])
            dl = kb.dma("y", buf, src[it], sem=f"q{pty}d", after=[t_send, ldp[0], ldp[1]])
            rd = vo("v", "tensor_reduce", [dl, t_pp[-1]], out=colp[:, 4:5], in_=buf,
                    axis=AXX, op=ALU.add, apply_absolute_value=True)
            ldp[pty] = rd
            t_pp.append(vo("v", "tensor_tensor", [rd], out=pacc[:, which:which + 1],
                           in0=pacc[:, which:which + 1], in1=colp[:, 4:5], op=ALU.add))
    t_pd = kb.dma("y", w0part, pacc[:], after=[t_pp[-1]])

    t_p7 = P.barrier()
    qt = []
    for which, (src, dst, nt, wd) in enumerate(((w1gu_q, w1gu_h, 16, DM),
                                                (w1pj_q, w1pj_h, 2, F0))):
        for it in range(nt):
            for ch in range(wd // 2048):
                pty = (it * (wd // 2048) + ch) % 2
                wbuf = q_w[pty]
                dl = kb.dma("y", wbuf, src[it, :, 2048 * ch:2048 * ch + 2048],
                            sem=f"q{pty}d", after=[t_p7] + qt[-1:])
                th = emit_tern_group(P, wbuf, q_wb, q_c2, q_tp, q_tn, q_ts, q_hat,
                                     q_sc, q_scb, q_th, q_tnh, 32, [dl] + qt[-1:])
                qt.append(kb.dma("g", dst[it, :, 2048 * ch:2048 * ch + 2048], q_hat,
                                 sem="qhd", after=[th]))

    t_fin = P.barrier()
    kb.fence("y", [t_fin])
    P.close()
    return P


def build_B():
    P = Prog()
    kb = P.kb
    mm, tr, vo, ac = P.mm, P.tr, P.vo, P.ac

    u8 = P.inp("u8", [8, 128, DM])
    yloc = P.inp("yloc", [128, 32, TOK])
    cumdt = P.inp("cumdt", [128, 32, TOK])
    zsil = P.inp("zsil", [128, 32, TOK], BF16)
    cmt = P.inp("cmt", [16, TOK])
    svp = P.inp("svp", [3, 128, 32, 16])
    pvp = P.inp("pvp", [3, 128, 32, 16])
    alog = P.inp("alog", [128, 32, 16])
    modr0 = P.inp("modr0", [1, 12288])
    modr1 = P.inp("modr1", [1, 12288])
    w0pg = P.inp("w0pg", [128, 8])
    w0pd = P.inp("w0pd", [128, 8])
    wout = P.inp("wout", [16, 128, 32, 128])
    bout = P.inp("bout", [128, 16])
    w0gu = P.inp("w0gu", [128, 128, 16, 128])
    w0dn = P.inp("w0dn", [16, 128, 64, 128])
    wqkv = P.inp("wqkv", [48, 128, 16, 128])
    wgate = P.inp("wgate", [16, 128, 16, 128])
    idn = P.inp("idn", [128, 128])
    dmt = P.inp("dmt", [128, 512])
    rampr = P.inp("rampr", [1, 256])
    rkdc = P.inp("rkdc", [128, 2])

    x3o = P.outp("x3o", [128, 16, TOK])
    ylao = P.outp("ylao", [128, 16, TOK])
    stoo = P.outp("stoo", [128, 16, HD])
    gsigo = P.outp("gsigo", [128, 16, TOK], BF16)
    qsilo = P.outp("qsilo", [128, 16, TOK], BF16)
    hmid_d = P.scratch("hmid_d", [64, 128, TOK])

    AR = P.sb("AR", [128, 16384])
    fl = AR.ap()

    def arv(a, b):
        return fl[:, a:b]

    uxbuf = P.sb("uxbuf", [128, 16, TOK])
    YBF = P.sb("YBF", [128, 32768], BF16)
    yb = YBF.ap()

    def ybv(a, b, t=None):
        v = yb[:, a:b]
        return v if t is None else v.rearrange("p (a b) -> p a b", a=t)

    id_s = P.sb("id_s", [128, 128])
    ones1 = P.sb("ones1", [1, 128])
    onesc = P.sb("onesc", [128, 1])
    colp = P.sb("colp", [128, 8])

    modc0 = P.sb("modc0", [128, 64])
    modc1 = P.sb("modc1", [128, 32])
    gacl = P.sb("gacl", [128, 16])
    gbcl = P.sb("gbcl", [128, 16])
    cbp1 = P.sb("cbp1", [128, 16])
    cap1 = P.sb("cap1", [128, 16])
    tmp16 = P.sb("tmp16", [128, 16])
    rrep = P.sb("rrep", [128, TOK])
    drep = P.sb("drep", [128, TOK])
    s0s = P.sb("s0s", [1, 2])
    s0v = P.sb("s0v", [2, 1])
    s0c = P.sb("s0c", [128, 4])
    w0g_s = P.sb("w0g_s", [128, 16])
    bout_c = P.sb("bout_c", [128, 16])
    ti_i = P.sb("ti_i", [128, 512], I32)
    rmax = P.sb("rmax", [128, 8])
    zs_b = P.sb("zs_b", [128, 512], BF16)
    gs_b = zs_b.ap()
    rampr_s = P.sb("rampr_s", [1, 256])
    rkdc_s = P.sb("rkdc_s", [128, 2])
    kdc = P.sb("kdc", [128, 2])
    st_f = [P.sb(f"st_f{i}", [64, 64]) for i in range(2)]
    wob = P.sb("wob", [128, 2, 128], BF16)

    u_s = arv(0, 2048)
    modm = fl[0:96, 15360:15488]
    sin_s = arv(13312, 13824).rearrange("p (a b) -> p a b", a=32)
    at_s = arv(13824, 14336).rearrange("p (a b) -> p a b", a=32)
    mrep = arv(8192, 9216)
    yl_t = [arv(0, 512), arv(512, 1024)]
    cd_t = [arv(1024, 1536), arv(1536, 2048)]
    zs32 = arv(2048, 2560)
    e_t = arv(2560, 3072)
    w8_t = arv(3072, 3584)
    acc_t = arv(3584, 4096)
    crp = arv(4096, 12288).rearrange("p (a b) -> p a b", a=16)
    wstr = [arv(4096, 8192).rearrange("p (a b) -> p a b", a=32),
            arv(8192, 12288).rearrange("p (a b) -> p a b", a=32)]
    cmt_s = fl[0:16, 12288:13312]
    sel16 = fl[0:16, 14336:16384]
    sel8 = fl[0:8, 7168:8192]
    rows = fl[0:1, 4096:8192]
    hm_t = [arv(0, 1024), arv(1024, 2048)]
    e2_t = arv(2048, 3072)
    r1_t = arv(3072, 4096)
    h2f = arv(12288, 13312)
    sq_t = arv(14336, 16384)
    tq_a = arv(0, 2048)
    tq_b = arv(2048, 4096)
    hmb_t = [arv(12288, 13312), arv(13312, 14336)]
    sil_t = arv(14336, 15360)
    up_t = arv(15360, 16384)
    i2ld = [arv(12288, 12544), arv(12544, 12800)]
    e2b = arv(12800, 13056)
    r1b = arv(13056, 13312)
    sqb = arv(13312, 13568)
    ti2 = arv(13568, 13824)
    qdrow = fl[0:1, 512:768]
    kf32 = fl[0:64, 3072:3200]
    dmt_s = arv(12288, 12800)
    ddh = arv(12800, 13312)
    ystg = [fl[0:64, 1024:2048], fl[0:64, 2048:3072]]
    ybf = ybv(0, 32768, 32)
    i1b = ybv(0, 16384, 16)
    i2h = ybv(0, 16384, 64)
    h3b = ybv(0, 16384, 16)
    wtb = [ybv(16384, 20480, 32), ybv(20480, 24576, 32)]
    dnb = [ybv(16384, 20480, 32), ybv(20480, 24576, 32)]
    qkvb = [ybv(16384, 19456, 3), ybv(19456, 22528, 3)]
    wqb = [ybv(22528, 24576, 16), ybv(24576, 26624, 16)]
    awtb = ybv(26624, 27136, 2)
    qdrb = yb[0:64, 27136:27392]
    ktokb = ybv(27392, 27520, 2)
    vtokb = ybv(27520, 27648, 2)
    qqb = yb[0:64, 27648:27904]
    st_b = yb[0:64, 27904:27968]
    qfb = yb[0:64, 28032:29056]
    kfb = yb[0:64, 29056:30080]
    vfb = yb[0:64, 30080:31104]

    ps = [P.ps(f"ps{i}", [128, 512]) for i in range(6)]
    ps2v = ps[2].ap().rearrange("p (a b) -> p a b", a=2)

    kb.dma("y", id_s[:], idn)
    kb.dma("y", modm[0:96, :], modr0.rearrange("o (q p) -> (o q) p", p=128))
    kb.dma("y", at_s, alog)
    kb.dma("y", sin_s, svp[0])
    st_sv1 = arv(12288, 12800).rearrange("p (a b) -> p a b", a=32)
    st_sv2 = arv(12800, 13312).rearrange("p (a b) -> p a b", a=32)
    st_pv0 = arv(14336, 14848).rearrange("p (a b) -> p a b", a=32)
    st_pv1 = arv(14848, 15360).rearrange("p (a b) -> p a b", a=32)
    kb.dma("y", st_sv1, svp[1])
    kb.dma("y", st_sv2, svp[2])
    kb.dma("y", st_pv0, pvp[0])
    kb.dma("y", st_pv1, pvp[1])
    kb.dma("y", w0g_s[:, 0:8], w0pg)
    kb.dma("y", w0g_s[:, 8:16], w0pd)
    kb.dma("y", bout_c[:], bout)
    kb.dma("y", rampr_s[:], rampr)
    kb.dma("y", rkdc_s[:], rkdc)
    t_c0 = P.barrier()
    vo("v", "memset", [t_c0], ap=ones1[:], constant=1.0)
    t_oc = vo("v", "memset", [t_c0], ap=onesc[:], constant=1.0)
    tk = tr(ps[4][0:128, 0:96], modm[0:96, :], id_s[0:96, 0:96], [t_c0])
    t_m0 = vo("v", "tensor_copy", [tk], out=modc0[:], in_=ps[4][:, 32:96])
    dmm = kb.dma("y", modm[0:96, :], modr1.rearrange("o (q p) -> (o q) p", p=128),
                 after=[tk])
    tk2 = tr(ps[4][0:128, 0:96], modm[0:96, :], id_s[0:96, 0:96], [dmm, t_m0])
    t_m1 = vo("v", "tensor_copy", [tk2], out=modc1[:], in_=ps[4][:, 0:32])

    def clamp(dst, src_ap, tprev):
        ta = vo("v", "tensor_single_scalar", [tprev], out=dst, in_=src_ap, scalar=0.0,
                op=ALU.is_ge)
        tb = vo("v", "tensor_scalar", [ta], out=dst, in0=dst, scalar1=2.0, scalar2=-1.0,
                op0=ALU.mult, op1=ALU.add)
        tm = vo("v", "scalar_tensor_tensor", [tprev, tb], out=tmp16[:], in0=src_ap,
                scalar=-1.0, in1=src_ap, op0=ALU.mult, op1=ALU.max)
        tm2 = vo("v", "tensor_scalar_max", [tm], out=tmp16[:], in0=tmp16[:], scalar1=0.25)
        return vo("v", "tensor_tensor", [tb, tm2], out=dst, in0=dst, in1=tmp16[:],
                  op=ALU.mult)

    t_ga = clamp(gacl[:], modc0[:, 0:16], t_m0)
    t_gb = clamp(gbcl[:], modc0[:, 48:64], t_ga)
    vo("v", "tensor_scalar_add", [t_m0], out=cbp1[:], in0=modc0[:, 32:48], scalar1=1.0)
    vo("v", "tensor_scalar_add", [t_m1], out=cap1[:], in0=modc1[:, 16:32], scalar1=1.0)
    t_at = ac([t_c0], out=at_s, in_=at_s, func=AF.Exp)
    t_at = vo("v", "tensor_scalar_mul", [t_at], out=at_s, in0=at_s, scalar1=-1.0)
    t1 = vo("v", "tensor_tensor", [t_c0], out=sq_t.rearrange("p (a b) -> p a b", a=32)[:, :, 0:16],
            in0=st_pv1, in1=st_sv2, op=ALU.mult) if False else None
    tmpv = arv(15488, 16000).rearrange("p (a b) -> p a b", a=32)
    t1 = vo("v", "tensor_tensor", [t_c0], out=tmpv, in0=st_pv1, in1=st_sv2, op=ALU.mult)
    t1 = vo("v", "tensor_tensor", [t1], out=tmpv, in0=tmpv, in1=st_sv1, op=ALU.add)
    t1 = vo("v", "tensor_tensor", [t1], out=tmpv, in0=tmpv, in1=st_pv0, op=ALU.mult)
    t_sin = vo("v", "tensor_tensor", [t1], out=sin_s, in0=sin_s, in1=tmpv,
               op=ALU.add)
    t_r0 = vo("v", "tensor_reduce", [t_c0], out=colp[:, 0:1], in_=w0g_s[:, 0:8],
              axis=AXX, op=ALU.add)
    t_r1 = vo("v", "tensor_reduce", [t_r0], out=colp[:, 1:2], in_=w0g_s[:, 8:16],
              axis=AXX, op=ALU.add)
    tmg = mm(ps[5][0:2, 0:1], colp[:, 0:2], onesc[:, 0:1], True, True, [t_r1, t_oc])
    t_s0 = vo("v", "tensor_copy", [tmg], out=s0v[0:2, 0:1], in_=ps[5][0:2, 0:1])
    tr0 = tr(ps[4][0:1, 0:2], s0v[0:2, 0:1], id_s[0:2, 0:2], [t_s0])
    t_s0 = vo("v", "tensor_copy", [tr0], out=s0s[0:1, 0:2], in_=ps[4][0:1, 0:2])
    t_s0 = vo("v", "tensor_scalar", [t_s0], out=s0s[0:1, 0:1], in0=s0s[0:1, 0:1],
              scalar1=1.0 / (2048.0 * 16384.0), scalar2=EPS_Q, op0=ALU.mult, op1=ALU.max)
    t_s0 = vo("v", "tensor_scalar", [t_s0], out=s0s[0:1, 1:2], in0=s0s[0:1, 1:2],
              scalar1=1.0 / (2048.0 * 8192.0), scalar2=EPS_Q, op0=ALU.mult, op1=ALU.max)
    tmr = mm(ps[5][:, 0:2], ones1[0:1, :], s0s[0:1, 0:2], True, True, [t_s0, tmg])
    t_s0c = vo("v", "tensor_copy", [tmr], out=s0c[:, 0:2], in_=ps[5][:, 0:2])
    t_s0c = vo("v", "tensor_scalar_mul", [t_s0c], out=s0c[:, 0:2], in0=s0c[:, 0:2],
               scalar1=0.5)
    t_s0c = vo("v", "tensor_scalar_mul", [t_s0c], out=s0c[:, 2:4], in0=s0c[:, 0:2],
               scalar1=-1.0)

    d_cmt = kb.dma("y", cmt_s, cmt, after=[t_sin], sem="cmd")
    t_sel16 = vo("v", "memset", [t_sin, t_s0c], ap=sel16, constant=1.0)
    t_sel16 = vo("v", "tensor_tensor", [t_sel16],
                 out=sel16.rearrange("p (a b) -> p a b", a=16),
                 in0=id_s[0:16, 0:16].unsqueeze(2).broadcast_to((16, 16, 128)),
                 in1=sel16.rearrange("p (a b) -> p a b", a=16), op=ALU.mult)

    t_b05 = P.barrier()
    prev_tr = None
    for tt8 in range(8):
        dl = kb.dma("y", u_s, u8[tt8], sem="u0d", after=[t_b05, prev_tr])
        for kf in range(16):
            tk = tr(ps[4][:, 0:128], u_s[:, 128 * kf:128 * kf + 128], id_s[:],
                    [dl, prev_tr])
            prev_tr = vo("v", "tensor_copy", [tk],
                         out=uxbuf[:, kf, 128 * tt8:128 * tt8 + 128], in_=ps[4][:, 0:128])
    t_uT = prev_tr

    t_b1 = P.barrier()
    lastld = {}
    crt_p = [None, None]
    for th in range(2):
        hw = 512 * th
        crt = []
        for n in range(16):
            pp = ps[2 + (n % 2)]
            tb = mm(pp[:, :], sel16[:, 128 * n:128 * n + 128], cmt_s[:, hw:hw + 512],
                    True, True, [t_b1, t_sel16, d_cmt, crt_p[n % 2]] +
                    ([lastld.get("crpfree")] if th > 0 else []))
            crt_p[n % 2] = vo("v", "tensor_copy", [tb], out=crp[:, n, :], in_=pp[:, :])
            crt.append(crt_p[n % 2])
        t_crp = crt[-1]
        for m in range(32):
            i2 = m % 2
            dly = kb.dma("y", yl_t[i2], yloc[:, m, hw:hw + 512], sem=f"x{i2}d",
                         after=[t_b1, lastld.get(("y", i2)), t_crp])
            dlc = kb.dma("s", cd_t[i2], cumdt[:, m, hw:hw + 512], sem=f"c{i2}d",
                         after=[t_b1, lastld.get(("c", i2)), t_crp])
            dlz = kb.dma("g", zs_b[:], zsil[:, m, hw:hw + 512], sem="z0d",
                         after=[t_b1, lastld.get("z"), t_crp])
            acc = None
            for n in range(16):
                e = ac([dlc, t_at] + ([acc] if acc else [t_crp, lastld.get("pt")]),
                       out=e_t, in_=cd_t[i2], func=AF.Exp, scale=at_s[:, m, n:n + 1])
                w8 = vo("v", "tensor_scalar", [e, t_sin], out=w8_t, in0=e_t,
                        scalar1=sin_s[:, m, n:n + 1], scalar2=None, op0=ALU.mult)
                t2 = vo("g", "tensor_tensor", [w8], out=e_t, in0=w8_t, in1=crp[:, n, :],
                        op=ALU.mult)
                if acc is None:
                    acc = vo("v", "tensor_copy", [t2], out=acc_t, in_=e_t)
                else:
                    acc = vo("v", "tensor_tensor", [t2, acc], out=acc_t, in0=acc_t,
                             in1=e_t, op=ALU.add)
            szc = vo("v", "tensor_copy", [dlz], out=zs32, in_=zs_b[:])
            tsum = vo("v", "tensor_tensor", [acc, dly], out=acc_t, in0=acc_t,
                      in1=yl_t[i2], op=ALU.add)
            tyb = vo("v", "tensor_tensor", [tsum, szc], out=ybf[:, m, hw:hw + 512],
                     in0=acc_t, in1=zs32, op=ALU.mult)
            lastld[("y", i2)] = tsum
            lastld[("c", i2)] = e
            lastld["z"] = szc
            lastld["pt"] = tyb
            if m == 31:
                lastld["crpfree"] = P.mark([acc, tsum])

    t_b2 = P.barrier()
    wl = [None, None]
    pev = {}
    cvp = [None, None]
    wob_prev = {0: [], 1: []}
    for f in range(16):
        i2 = f % 2
        dl = kb.dma("y", wstr[i2].rearrange("p a b -> p (a b)"),
                    wout[f].rearrange("p a b -> p (a b)"), sem=f"w{i2}d",
                    after=[t_b2, wl[i2]])
        toks = {0: [], 1: []}
        for k in range(32):
            kp = k % 2
            cvp[kp] = vo("g", "tensor_copy", [dl] +
                         ([toks[0][-2], toks[1][-2]] if k >= 2
                          else [t_b2, wl[i2]] + wob_prev[kp]),
                         out=wob[:, kp, :], in_=wstr[i2][:, k, :])
            for ch in range(2):
                toks[ch].append(mm(ps[ch][:, :], wob[:, kp, :],
                                   ybf[:, k, 512 * ch:512 * ch + 512], k == 0, k == 31,
                                   [cvp[kp], t_b2] + ([pev[ch]] if (k == 0 and ch in pev) else [])))
        wl[i2] = toks[1][-1]
        wob_prev[0] = [toks[0][30], toks[1][30]]
        wob_prev[1] = [toks[0][31], toks[1][31]]
        for ch in range(2):
            t1 = vo("v", "tensor_scalar", [toks[ch][-1]],
                    out=arv(12288 + 512 * ch, 12800 + 512 * ch),
                    in0=ps[ch][:, :], scalar1=bout_c[:, f:f + 1], scalar2=gacl[:, f:f + 1],
                    op0=ALU.add, op1=ALU.mult)
            pev[ch] = vo("v", "tensor_tensor", [t1, t_uT],
                         out=uxbuf[:, f, 512 * ch:512 * ch + 512],
                         in0=uxbuf[:, f, 512 * ch:512 * ch + 512],
                         in1=arv(12288 + 512 * ch, 12800 + 512 * ch), op=ALU.add)

    def ln_stats(eps, after):
        toks_s = []
        toks_q = []
        for ch in range(2):
            for k in range(16):
                toks_s.append(mm(ps[2][0:1, :], onesc[:, 0:1],
                                 uxbuf[:, k, 512 * ch:512 * ch + 512], k == 0, k == 15,
                                 after + [t_oc] + toks_s[-1:]))
                sq = ac(after + toks_q[-1:], out=sq_t[:, 0:512],
                        in_=uxbuf[:, k, 512 * ch:512 * ch + 512], func=AF.Square)
                toks_q.append(mm(ps[3][0:1, :], onesc[:, 0:1], sq_t[:, 0:512],
                                 k == 0, k == 15, [sq, t_oc] + toks_q[-1:]))
            toks_s.append(vo("v", "tensor_copy", [toks_s[-1]],
                             out=rows[0:1, 1024 + 512 * ch:1536 + 512 * ch],
                             in_=ps[2][0:1, :]))
            toks_q.append(vo("v", "tensor_copy", [toks_q[-1]],
                             out=rows[0:1, 2048 + 512 * ch:2560 + 512 * ch],
                             in_=ps[3][0:1, :]))
        tmn = vo("v", "tensor_scalar_mul", [toks_s[-1]], out=rows[0:1, 1024:2048],
                 in0=rows[0:1, 1024:2048], scalar1=1.0 / DM)
        tv = vo("v", "tensor_scalar_mul", [toks_q[-1]], out=rows[0:1, 2048:3072],
                in0=rows[0:1, 2048:3072], scalar1=1.0 / DM)
        tms = vo("v", "tensor_tensor", [tmn], out=rows[0:1, 0:1024],
                 in0=rows[0:1, 1024:2048], in1=rows[0:1, 1024:2048], op=ALU.mult)
        tv = vo("v", "tensor_tensor", [tv, tms], out=rows[0:1, 2048:3072],
                in0=rows[0:1, 2048:3072], in1=rows[0:1, 0:1024], op=ALU.subtract)
        tsd = ac([tv], out=rows[0:1, 2048:3072], in_=rows[0:1, 2048:3072], func=AF.Sqrt,
                 bias=eps)
        trc = vo("v", "reciprocal", [tsd], out=rows[0:1, 2048:3072],
                 in_=rows[0:1, 2048:3072])
        tmr_ = vo("v", "tensor_tensor", [trc], out=rows[0:1, 1024:2048],
                  in0=rows[0:1, 1024:2048], in1=rows[0:1, 2048:3072], op=ALU.mult)
        outs = []
        for ch in range(2):
            t5 = mm(ps[2][:, :], ones1[0:1, :], rows[0:1, 2048 + 512 * ch:2560 + 512 * ch],
                    True, True, [trc] + outs)
            outs.append(vo("v", "tensor_copy", [t5], out=rrep[:, 512 * ch:512 * ch + 512],
                           in_=ps[2][:, :]))
            t6 = mm(ps[3][:, :], ones1[0:1, :], rows[0:1, 1024 + 512 * ch:1536 + 512 * ch],
                    True, True, [tmr_] + outs)
            outs.append(vo("v", "tensor_copy", [t6], out=mrep[:, 512 * ch:512 * ch + 512],
                           in_=ps[3][:, :]))
        return outs[-1]

    t_b3 = P.barrier()
    t_ln2 = ln_stats(EPS_LN, [t_b3])
    tz = vo("v", "memset", [t_b3], ap=rmax[:], constant=0.0)
    h2toks = []
    trp = tz
    for k in range(16):
        t1 = vo("v", "tensor_tensor", [t_ln2] + h2toks[-1:], out=h2f, in0=uxbuf[:, k, :],
                in1=rrep[:], op=ALU.mult)
        t1 = vo("v", "tensor_tensor", [t1], out=h2f, in0=h2f, in1=mrep[:], op=ALU.subtract)
        t1 = vo("v", "tensor_scalar", [t1], out=h2f, in0=h2f, scalar1=cbp1[:, k:k + 1],
                scalar2=modc0[:, 16 + k:17 + k], op0=ALU.mult, op1=ALU.add)
        dst = kb.dma("g", hmid_d[k], h2f, sem="h2d", after=[t1])
        for c8 in range(8):
            tk = tr(ps[4][:, 0:128], h2f[:, 128 * c8:128 * c8 + 128], id_s[:], [t1, trp])
            sq = ac([tk], out=sq_t[:, 0:128], in_=ps[4][:, 0:128], func=AF.Square)
            trp = vo("v", "tensor_reduce", [sq, tz], out=colp[:, 4:5], in_=sq_t[:, 0:128],
                     axis=AXX, op=ALU.max)
            trp = vo("v", "tensor_tensor", [trp], out=rmax[:, c8:c8 + 1],
                     in0=rmax[:, c8:c8 + 1], in1=colp[:, 4:5], op=ALU.max)
        h2toks.append(dst)
    t_mx = ac([trp], out=rmax[:], in_=rmax[:], func=AF.Sqrt)
    rows8 = fl[0:8, 4096:4480]

    def make_scales(mx_tok, s0idx, after):
        ts8 = vo("v", "memset", [mx_tok] + after, ap=sel8, constant=1.0)
        ts8 = vo("v", "tensor_tensor", [ts8], out=sel8.rearrange("p (a b) -> p a b", a=8),
                 in0=id_s[0:8, 0:8].unsqueeze(2).broadcast_to((8, 8, 128)),
                 in1=sel8.rearrange("p (a b) -> p a b", a=8), op=ALU.mult)
        tt_ = tr(ps[4][0:8, 0:128], rmax[:], id_s[:], [mx_tok] + after)
        t1 = vo("v", "tensor_copy", [tt_], out=rows8[:, 0:128], in_=ps[4][0:8, 0:128])
        t1 = vo("v", "tensor_scalar_max", [t1], out=rows8[:, 0:128], in0=rows8[:, 0:128],
                scalar1=EPS_Q)
        t2 = vo("v", "reciprocal", [t1], out=rows8[:, 128:256], in_=rows8[:, 0:128])
        t3 = vo("v", "tensor_tensor", [t2], out=rows8[:, 256:384], in0=rows8[:, 0:128],
                in1=rows8[:, 128:256], op=ALU.mult)
        t3 = vo("v", "tensor_scalar", [t3], out=rows8[:, 256:384], in0=rows8[:, 256:384],
                scalar1=-1.0, scalar2=2.0, op0=ALU.mult, op1=ALU.add)
        t3 = vo("v", "tensor_tensor", [t3], out=rows8[:, 128:256], in0=rows8[:, 128:256],
                in1=rows8[:, 256:384], op=ALU.mult)
        tsr = vo("v", "tensor_scalar_mul", [t3], out=rows8[:, 128:256],
                 in0=rows8[:, 128:256], scalar1=127.0)
        tdr = vo("v", "tensor_scalar", [t3, tsr], out=rows8[:, 0:128],
                 in0=rows8[:, 0:128], scalar1=s0c[0:8, s0idx:s0idx + 1],
                 scalar2=2.0 / 127.0, op0=ALU.mult, op1=ALU.mult)
        reps = []
        for c8 in range(8):
            t5 = mm(ps[2][:, 0:128], sel8[:, 128 * c8:128 * c8 + 128],
                    rows8[:, 128:256], True, True, [tsr, ts8] + reps[-1:])
            reps.append(vo("v", "tensor_copy", [t5],
                           out=rrep[:, 128 * c8:128 * c8 + 128], in_=ps[2][:, 0:128]))
            t6 = mm(ps[3][:, 0:128], sel8[:, 128 * c8:128 * c8 + 128],
                    rows8[:, 0:128], True, True, [tdr, ts8] + reps[-1:])
            reps.append(vo("v", "tensor_copy", [t6],
                           out=drep[:, 128 * c8:128 * c8 + 128], in_=ps[3][:, 0:128]))
        return reps[-1]

    t_sc = make_scales(t_mx, 0, [t_ln2])
    qts = []
    ldh = [None, None]
    for k in range(16):
        i2 = k % 2
        dl = kb.dma("y", hm_t[i2], hmid_d[k], sem=f"x{i2}d",
                    after=[t_sc, h2toks[k]] + ([ldh[i2]] if ldh[i2] else []))
        for hf in range(2):
            ty = vo("v", "tensor_tensor", [dl, t_sc] + qts[-1:], out=e2_t[:, 0:512],
                    in0=hm_t[i2][:, 512 * hf:512 * hf + 512],
                    in1=rrep[:, 512 * hf:512 * hf + 512], op=ALU.mult)
            qts.append(emit_round(P, e2_t[:, 0:512], i1b[:, k, 512 * hf:512 * hf + 512],
                                  ti_i[:], r1_t[:, 0:512], sq_t[:, 0:512], [ty]))
        ldh[i2] = qts[-1]

    t_b4 = P.barrier()
    tz2 = vo("v", "memset", [t_b4], ap=rmax[:], constant=0.0)
    trp = tz2
    wl = [None, None]
    hdma = [None, None]
    pev = {}
    for mp in range(64):
        i2 = mp % 2
        wfl = wstr[i2].rearrange("p a b -> p (a b)")
        dg = kb.dma("y", wstr[i2][:, 0:16, :].rearrange("p a b -> p (a b)"),
                    w0gu[mp].rearrange("p a b -> p (a b)"), sem=f"w{i2}d",
                    after=[t_b4, wl[i2]])
        dup = kb.dma("s", wstr[i2][:, 16:32, :].rearrange("p a b -> p (a b)"),
                     w0gu[64 + mp].rearrange("p a b -> p (a b)"), sem=f"v{i2}d",
                     after=[t_b4, wl[i2]])
        tq = [dg, dup]
        tlast = None
        for hf in range(2):
            tn = vo("v", "tensor_scalar", tq + [pev.get("q")], out=tq_a,
                    in0=wfl[:, 2048 * hf:2048 * hf + 2048], scalar1=s0c[:, 2:3],
                    scalar2=None, op0=ALU.is_lt)
            tp = vo("v", "tensor_scalar", tq + [tn], out=tq_b,
                    in0=wfl[:, 2048 * hf:2048 * hf + 2048], scalar1=s0c[:, 0:1],
                    scalar2=None, op0=ALU.is_gt)
            tlast = vo("v", "tensor_tensor", [tn, tp],
                       out=wtb[i2].rearrange("p a b -> p (a b)")[:, 2048 * hf:2048 * hf + 2048],
                       in0=tq_b, in1=tq_a, op=ALU.subtract)
            pev["q"] = tlast
        wl[i2] = tlast
        evs = []
        for ch in range(2):
            tg = []
            tu = []
            for k in range(16):
                tg.append(mm(ps[0 + (mp % 2) * 0][:, :] if False else ps[0][:, :],
                             wtb[i2][:, k, :], i1b[:, k, 512 * ch:512 * ch + 512],
                             k == 0, k == 15, [tlast] + ([pev["g"]] if "g" in pev else [])))
                tu.append(mm(ps[1][:, :], wtb[i2][:, 16 + k, :],
                             i1b[:, k, 512 * ch:512 * ch + 512], k == 0, k == 15,
                             [tlast] + ([pev["u"]] if "u" in pev else [])))
            t1 = vo("v", "tensor_tensor", [tg[-1], pev.get("ev")], out=sil_t[:, 0:512],
                    in0=ps[0][:, 0:512], in1=drep[:, 512 * ch:512 * ch + 512],
                    op=ALU.mult)
            t1 = ac([t1], out=sil_t[:, 0:512], in_=sil_t[:, 0:512], func=AF.Silu)
            pev["g"] = t1
            t2 = vo("v", "tensor_tensor", [tu[-1], pev.get("ev")], out=up_t[:, 0:512],
                    in0=ps[1][:, 0:512], in1=drep[:, 512 * ch:512 * ch + 512],
                    op=ALU.mult)
            pev["u"] = t2
            evs.append(vo("v", "tensor_tensor", [t1, t2, hdma[i2]],
                          out=hmb_t[i2][:, 512 * ch:512 * ch + 512], in0=sil_t[:, 0:512],
                          in1=up_t[:, 0:512], op=ALU.mult))
            pev["ev"] = evs[-1]
        hdma[i2] = kb.dma("g", hmid_d[mp], hmb_t[i2], sem=f"h{i2}d", after=evs)
        for c8 in range(8):
            tk = tr(ps[4][:, 0:128], hmb_t[i2][:, 128 * c8:128 * c8 + 128], id_s[:],
                    [evs[c8 // 4], trp])
            sq = ac([tk], out=rrep[:, 0:128], in_=ps[4][:, 0:128], func=AF.Square)
            trp = vo("v", "tensor_reduce", [sq], out=colp[:, 4:5], in_=rrep[:, 0:128],
                     axis=AXX, op=ALU.max)
            trp = vo("v", "tensor_tensor", [trp], out=rmax[:, c8:c8 + 1],
                     in0=rmax[:, c8:c8 + 1], in1=colp[:, 4:5], op=ALU.max)
    t_mx2 = ac([trp], out=rmax[:], in_=rmax[:], func=AF.Sqrt)

    t_b5 = P.barrier()
    t_sc2 = make_scales(t_mx2, 1, [t_b5])
    dnt = [None, None]
    ldq = [None, None]
    pdn = {}
    for tq4 in range(4):
        qw = 256 * tq4
        iqt = []
        for f2 in range(64):
            i2 = f2 % 2
            dl = kb.dma("y", i2ld[i2], hmid_d[f2, :, qw:qw + 256], sem=f"x{i2}d",
                        after=[t_sc2, ldq[i2], hdma[f2 % 2]])
            ty = vo("v", "tensor_tensor", [dl, t_sc2] + iqt[-1:], out=e2b,
                    in0=i2ld[i2], in1=rrep[:, qw:qw + 256], op=ALU.mult)
            iqt.append(emit_round(P, e2b, i2h[:, f2, :], ti_i[:, 0:256],
                                  r1b, sqb, [ty]))
            ldq[i2] = iqt[-1]
        t_i2 = iqt[-1]
        for f in range(16):
            i2 = f % 2
            dnl = []
            for kh in range(2):
                dd = kb.dma("y", wstr[i2][:, :, :].rearrange("p a b -> p (a b)"),
                            w0dn[f, :, 32 * kh:32 * kh + 32, :].rearrange("p a b -> p (a b)"),
                            sem=f"w{i2}d", after=[t_b5, dnt[i2]] + iqt[-1:])
                tn = vo("v", "tensor_scalar", [dd, pdn.get(("n", i2))], out=tq_a,
                        in0=wstr[i2].rearrange("p a b -> p (a b)")[:, 0:2048],
                        scalar1=s0c[:, 3:4], scalar2=None, op0=ALU.is_lt) if False else None
                tl = None
                for hf in range(2):
                    tn = vo("v", "tensor_scalar", [dd, pdn.get("q2")],
                            out=tq_a, in0=wstr[i2].rearrange("p a b -> p (a b)")[:, 2048 * hf:2048 * hf + 2048],
                            scalar1=s0c[:, 3:4], scalar2=None, op0=ALU.is_lt)
                    tp = vo("v", "tensor_scalar", [dd, tn], out=tq_b,
                            in0=wstr[i2].rearrange("p a b -> p (a b)")[:, 2048 * hf:2048 * hf + 2048],
                            scalar1=s0c[:, 1:2], scalar2=None, op0=ALU.is_gt)
                    tl = vo("v", "tensor_tensor", [tn, tp],
                            out=dnb[i2].rearrange("p a b -> p (a b)")[:, 2048 * hf:2048 * hf + 2048],
                            in0=tq_b, in1=tq_a, op=ALU.subtract)
                    pdn["q2"] = tl
                toks = []
                for k in range(32):
                    toks.append(mm(ps[5][:, 0:256], dnb[i2][:, k, :],
                                   i2h[:, 32 * kh + k, :], (kh == 0 and k == 0),
                                   (kh == 1 and k == 31),
                                   [tl, t_i2] + ([pdn[("e", 0)]] if (kh == 0 and k == 0 and ("e", 0) in pdn) else [])))
                dnl.append(toks[-1])
                dnt[i2] = toks[-1]
            t1 = vo("v", "tensor_tensor", [dnl[-1]], out=e2_t[:, 0:256],
                    in0=ps[5][:, 0:256], in1=drep[:, qw:qw + 256], op=ALU.mult)
            t1 = vo("v", "tensor_scalar", [t1], out=e2_t[:, 0:256], in0=e2_t[:, 0:256],
                    scalar1=gbcl[:, f:f + 1], scalar2=None, op0=ALU.mult)
            pdn[("e", 0)] = vo("v", "tensor_tensor", [t1], out=uxbuf[:, f, qw:qw + 256],
                               in0=uxbuf[:, f, qw:qw + 256], in1=e2_t[:, 0:256],
                               op=ALU.add)

    t_b7 = P.barrier()
    x3d = []
    for k in range(16):
        x3d.append(kb.dma("g", x3o[:, k, :], uxbuf[:, k, :], sem="x3d", after=[t_b7]))
    t_ln3 = ln_stats(EPS_LN, [t_b7])
    h3t = []
    for k in range(16):
        t1 = vo("v", "tensor_tensor", [t_ln3] + h3t[-1:], out=h2f, in0=uxbuf[:, k, :],
                in1=rrep[:], op=ALU.mult)
        t1 = vo("v", "tensor_tensor", [t1], out=h2f, in0=h2f, in1=mrep[:], op=ALU.subtract)
        h3t.append(vo("v", "tensor_scalar", [t1], out=h3b[:, k, :], in0=h2f,
                      scalar1=cap1[:, k:k + 1], scalar2=modc1[:, k:k + 1], op0=ALU.mult,
                      op1=ALU.add))
    t_h3 = h3t[-1]

    wl = [None, None]
    gst = None
    pev = {}
    for f in range(16):
        i2 = f % 2
        dl = kb.dma("y", wstr[i2][:, 0:16, :].rearrange("p a b -> p (a b)"),
                    wgate[f].rearrange("p a b -> p (a b)"), sem=f"w{i2}d",
                    after=[t_h3, wl[i2]])
        cb = vo("g", "tensor_copy", [dl, wl[i2], t_h3], out=wqb[i2], in_=wstr[i2][:, 0:16, :])
        evs = []
        for ch in range(2):
            toks = []
            for k in range(16):
                toks.append(mm(ps[ch][:, :], wqb[i2][:, k, :],
                               h3b[:, k, 512 * ch:512 * ch + 512], k == 0, k == 15,
                               [cb] + ([pev[ch]] if ch in pev else [])))
            ev = ac([toks[-1], gst], out=gs_b[:, 0:512], in_=ps[ch][:, :],
                    func=AF.Sigmoid)
            pev[ch] = ev
            gst = kb.dma("g", gsigo[:, f, 512 * ch:512 * ch + 512], gs_b[:, 0:512],
                         sem="gsd", after=[ev])
            evs.append(ev)
        wl[i2] = toks[-1]

    t_dmt = kb.dma("y", dmt_s, dmt, sem="dmd", after=[t_h3])
    qkd = [None, None]
    stod = None
    ylad = None
    qsd = None
    pq = {}
    for hp in range(16):
        i2 = hp % 2
        qtoks = []
        for j in range(3):
            m48 = 3 * hp + j
            dl = kb.dma("y", wstr[i2][:, 16:32, :].rearrange("p a b -> p (a b)"),
                        wqkv[m48].rearrange("p a b -> p (a b)"), sem=f"u{i2}d",
                        after=[t_b7, pq.get(("w", i2)), t_h3])
            cb = vo("g", "tensor_copy", [dl, pq.get(("w", i2))], out=wqb[i2],
                    in_=wstr[i2][:, 16:32, :])
            for ch in range(2):
                toks = []
                for k in range(16):
                    toks.append(mm(ps[ch][:, :], wqb[i2][:, k, :],
                                   h3b[:, k, 512 * ch:512 * ch + 512], k == 0, k == 15,
                                   [cb] + ([pq[("p", ch)]] if ("p", ch) in pq else [])))
                pq[("p", ch)] = ac([toks[-1], qkd[i2]],
                                   out=qkvb[i2][:, j, 512 * ch:512 * ch + 512],
                                   in_=ps[ch][:, :], func=AF.Silu)
                qtoks.append(pq[("p", ch)])
            pq[("w", i2)] = toks[-1]
        t_qkv = qtoks[-1]
        qs1 = kb.dma("g", qsilo[0:64, hp, :], qkvb[i2][0:64, 0, :], sem="qsd",
                     after=qtoks)
        qs2 = kb.dma("g", qsilo[64:128, hp, :], qkvb[i2][64:128, 1, :], sem="qtd",
                     after=qtoks)
        qkd[i2] = P.mark(qtoks + [qs1, qs2])
        for hh in (2 * hp, 2 * hp + 1):
            sh = float(SLOPES[hh])
            if hh % 2 == 0:
                srcs = (qkvb[i2][0:64, 0, :], qkvb[i2][64:128, 0, :], qkvb[i2][0:64, 1, :])
            else:
                srcs = (qkvb[i2][64:128, 1, :], qkvb[i2][0:64, 2, :], qkvb[i2][64:128, 2, :])
            q_fm, k_fm, v_fm = qfb, kfb, vfb
            dq = kb.dma("s", qfb, srcs[0], sem="qfd", after=qtoks + [pq.get("hdone")])
            dk = kb.dma("s", kfb, srcs[1], sem="kfd", after=qtoks + [pq.get("hdone")])
            dv = kb.dma("s", vfb, srcs[2], sem="vfd", after=qtoks + [pq.get("hdone")])
            t_qkv2 = P.mark([dq, dk, dv])
            tdd = ac([t_dmt, pq.get("dd"), t_qkv2], out=ddh, in_=dmt_s, func=AF.Exp,
                     scale=sh)
            tkd = ac([t_c0, pq.get("kd")], out=kdc[:], in_=rkdc_s[:], func=AF.Exp,
                     scale=sh)
            tqd = ac([t_c0, pq.get("qd")], out=qdrow, in_=rampr_s[:], func=AF.Exp,
                     scale=-sh)
            tqr = mm(ps[5][0:64, 0:256], ones1[0:1, 0:64], qdrow, True, True,
                     [tqd, pq.get("qr")])
            tqrb = vo("v", "tensor_copy", [tqr], out=qdrb[:], in_=ps[5][0:64, 0:256])
            pq["qd"] = tqrb
            pq["qr"] = tqrb
            tz1 = vo("v", "memset", [pq.get("st0"), stod], ap=st_f[0][:], constant=0.0)
            tzb = vo("v", "memset", [pq.get("stb")], ap=st_b[:], constant=0.0)
            cur = 0
            yst = []
            for i in range(NBLK):
                c0 = 256 * i
                tks = []
                for jt in range(2):
                    cpk = vo("v", "tensor_copy", [t_qkv2, pq.get("tr")], out=kf32,
                             in_=k_fm[:, c0 + 128 * jt:c0 + 128 * jt + 128])
                    tkk = tr(ps[4][0:128, 0:64], kf32, id_s[0:64, 0:64], [cpk])
                    tkv = vo("v", "tensor_scalar", [tkk, tkd, pq.get(("ktok", jt))],
                             out=ktokb[:, jt, :], in0=ps[4][:, 0:64],
                             scalar1=kdc[:, jt:jt + 1], scalar2=None, op0=ALU.mult)
                    cpv = vo("v", "tensor_copy", [t_qkv2, tkk, tkv], out=kf32,
                             in_=v_fm[:, c0 + 128 * jt:c0 + 128 * jt + 128])
                    tvv = tr(ps[4][0:128, 0:64], kf32, id_s[0:64, 0:64], [cpv, tkv])
                    tvc = vo("v", "tensor_copy", [tvv, pq.get(("vtok", jt))],
                             out=vtokb[:, jt, :], in_=ps[4][:, 0:64])
                    pq["tr"] = tvc
                    pq[("ktok", jt)] = tvc
                    pq[("vtok", jt)] = tvc
                    tks.append(tvc)
                awm = []
                for jt in range(2):
                    ta = mm(ps2v[:, jt, :], k_fm[:, c0 + 128 * jt:c0 + 128 * jt + 128],
                            q_fm[:, c0:c0 + 256], True, True,
                            [t_qkv2, pq.get(("aw", jt))])
                    awm.append(vo("v", "tensor_tensor", [ta, tdd], out=awtb[:, jt, :],
                                  in0=ps2v[:, jt, :], in1=ddh.rearrange("p (a b) -> p a b", a=2)[:, jt, :],
                                  op=ALU.mult))
                    pq[("aw", jt)] = awm[-1]
                tqq = vo("g", "tensor_tensor", [t_qkv2, tqrb, pq.get("qq")], out=qqb[:],
                         in0=q_fm[:, c0:c0 + 256], in1=qdrb[:], op=ALU.mult)
                ym = mm(ps[3][0:64, 0:256], vtokb[:, 0, :], awtb[:, 0, :], True, False,
                        [tks[-1], awm[0], pq.get("ye")])
                ym = mm(ps[3][0:64, 0:256], vtokb[:, 1, :], awtb[:, 1, :], False, False,
                        [awm[1], ym])
                ym = mm(ps[3][0:64, 0:256], st_b[:], qqb[:], False, True,
                        [tqq, ym, (tzb if i == 0 else pq.get("stb"))])
                ye = vo("v", "tensor_copy", [ym], out=ystg[hh % 2][:, c0:c0 + 256],
                        in_=ps[3][0:64, 0:256])
                pq["ye"] = ye
                pq["qq"] = ym
                yst.append(ye)
                kvm = mm(ps[5][0:64, 0:64], ktokb[:, 0, :], vtokb[:, 0, :], True, False,
                         [tks[-1], pq.get("kve")])
                kvm = mm(ps[5][0:64, 0:64], ktokb[:, 1, :], vtokb[:, 1, :], False, True,
                         [kvm])
                tst = vo("v", "scalar_tensor_tensor", [kvm, (tz1 if i == 0 else pq.get("st0"))],
                         out=st_f[1 - cur][:], in0=st_f[cur][:], scalar=float(BD[hh]),
                         in1=ps[5][0:64, 0:64], op0=ALU.mult, op1=ALU.add)
                pq["kve"] = tst
                pq["st0"] = tst
                cur = 1 - cur
                if i < NBLK - 1:
                    pq["stb"] = vo("v", "tensor_copy", [tst], out=st_b[:],
                                   in_=st_f[cur][:])
            stod = kb.dma("g", stoo[64 * (hh % 2):64 * (hh % 2) + 64, hh // 2, :],
                          st_f[cur][:], sem="std", after=[tst])
            ylad = kb.dma("g", ylao[64 * (hh % 2):64 * (hh % 2) + 64, hh // 2, :],
                          ystg[hh % 2][:, :], sem="yad", after=yst + [ylad])
            pq["dd"] = awm[-1]
            pq["kd"] = tks[-1]
            pq["hdone"] = P.mark([pq["ye"], pq["st0"], pq["qq"], awm[-1], tks[-1]])

    t_fin = P.barrier()
    kb.fence("y", [t_fin])
    P.close()
    return P


def build_C():
    P = Prog()
    kb = P.kb
    mm, tr, vo, ac = P.mm, P.tr, P.vo, P.ac

    x3i = P.inp("x3i", [128, 16, TOK])
    ylai = P.inp("ylai", [128, 16, TOK])
    qsili = P.inp("qsili", [128, 16, TOK], BF16)
    gsigi = P.inp("gsigi", [128, 16, TOK], BF16)
    stp = P.inp("stp", [3, 128, 16, HD])
    modr1 = P.inp("modr1", [1, 12288])
    normw = P.inp("normw", [128, 16])
    w1out = P.inp("w1out", [16, 128, 16, 128])
    wgu = P.inp("wgu", [128, 128, 16, 128], BF16)
    wpj = P.inp("wpj", [16, 128, 64, 128], BF16)
    ramp2 = P.inp("ramp2", [2, TOK])
    bd4c = P.inp("bd4c", [128, 16])
    selq = P.inp("selq", [2, 128])
    slpc = P.inp("slpc", [2, 16])
    idn = P.inp("idn", [128, 128])

    outo = P.outp("outo", [128, 16, TOK])
    hm_d = P.scratch("hm_d", [64, 128, TOK])

    AR = P.sb("AR", [128, 5120])
    fl = AR.ap()

    def arv(a, b):
        return fl[:, a:b]

    wstr = [arv(0, 2048).rearrange("p (a b) -> p a b", a=16),
            arv(2048, 4096).rearrange("p (a b) -> p a b", a=16)]
    rrep = arv(4096, 5120)
    mrep = arv(0, 1024)
    rows = fl[0:1, 0:4096]
    modm = fl[0:96, 0:128]
    qd2 = fl[0:2, 4096:5120]

    XY = P.sb("XY", [128, 16, TOK])
    YBF = P.sb("YBF", [128, 32768], BF16)
    yb = YBF.ap()
    gyn = yb[:, 0:16384].rearrange("p (a b) -> p a b", a=16)
    h4b = yb[:, 16384:32768].rearrange("p (a b) -> p a b", a=16)
    qfb = yb[:, 16384:17408]
    hnb = yb[:, 0:32768].rearrange("p (a b) -> p a b", a=64)
    SC = P.sb("SC", [128, 4096])
    sc = SC.ap()
    stps = sc[:, 0:3072].rearrange("p (a b c) -> p a b c", a=3, b=16)
    sin_f = sc[:, 3072:4096].rearrange("p (a b) -> p a b", a=16)

    id_s = P.sb("id_s", [128, 128])
    ones1 = P.sb("ones1", [1, 128])
    onesc = P.sb("onesc", [128, 1])
    modcC = P.sb("modcC", [128, 64])
    gacl = P.sb("gacl", [128, 16])
    gbcl = P.sb("gbcl", [128, 16])
    cbp1 = P.sb("cbp1", [128, 16])
    tmp16 = P.sb("tmp16", [128, 16])
    normw_s = P.sb("normw_s", [128, 16])
    bd4_s = P.sb("bd4_s", [128, 16])
    selq_s = P.sb("selq_s", [2, 128])
    slpc_s = P.sb("slpc_s", [2, 16])
    ramp_s = P.sb("ramp_s", [2, TOK])
    qsb = P.sb("qsb", [128, TOK], BF16)
    gsb = qsb.ap()
    sinb = P.sb("sinb", [128, 16, 128], BF16)
    wob = P.sb("wob", [128, 2, 128], BF16)
    WW = P.sb("WW", [128, 2, 32, 128], BF16)
    ww = WW.ap()
    WGB = ww[:, :, 0:16, :]
    WUB = ww[:, :, 16:32, :]
    WPB = ww

    ps = [P.ps(f"ps{i}", [128, 512]) for i in range(6)]

    kb.dma("y", id_s[:], idn)
    kb.dma("y", modm, modr1.rearrange("o (q p) -> (o q) p", p=128))
    kb.dma("y", normw_s[:], normw)
    kb.dma("y", bd4_s[:], bd4c)
    kb.dma("y", selq_s[:], selq)
    kb.dma("y", slpc_s[:], slpc)
    kb.dma("y", ramp_s[:], ramp2)
    for i in range(3):
        kb.dma("s", stps[:, i].rearrange("p a b -> p (a b)"),
               stp[i].rearrange("p a b -> p (a b)"))
    yld = []
    for k in range(16):
        yld.append(kb.dma("g", XY[:, k, :], ylai[:, k, :], sem="yld"))
    t_c0 = P.barrier()
    vo("v", "memset", [t_c0], ap=ones1[:], constant=1.0)
    t_oc = vo("v", "memset", [t_c0], ap=onesc[:], constant=1.0)
    tk = tr(ps[4][0:128, 0:96], modm[0:96, :], id_s[0:96, 0:96], [t_c0])
    t_m = vo("v", "tensor_copy", [tk], out=modcC[:], in_=ps[4][:, 32:96])

    def clamp(dst, src_ap, tprev):
        ta = vo("v", "tensor_single_scalar", [tprev], out=dst, in_=src_ap, scalar=0.0,
                op=ALU.is_ge)
        tb = vo("v", "tensor_scalar", [ta], out=dst, in0=dst, scalar1=2.0, scalar2=-1.0,
                op0=ALU.mult, op1=ALU.add)
        tm = vo("v", "scalar_tensor_tensor", [tprev, tb], out=tmp16[:], in0=src_ap,
                scalar=-1.0, in1=src_ap, op0=ALU.mult, op1=ALU.max)
        tm2 = vo("v", "tensor_scalar_max", [tm], out=tmp16[:], in0=tmp16[:], scalar1=0.25)
        return vo("v", "tensor_tensor", [tb, tm2], out=dst, in0=dst, in1=tmp16[:],
                  op=ALU.mult)

    t_ga = clamp(gacl[:], modcC[:, 0:16], t_m)
    t_gb = clamp(gbcl[:], modcC[:, 48:64], t_ga)
    t_cb = vo("v", "tensor_scalar_add", [t_m], out=cbp1[:], in0=modcC[:, 32:48],
              scalar1=1.0)
    t_sz = vo("v", "memset", [t_c0], ap=sinb.ap().rearrange("p a b -> p (a b)"),
              constant=0.0)
    t_sinb = t_sz
    for k in range(16):
        t1 = vo("v", "tensor_scalar", [t_c0, t_sinb], out=sin_f[:, k, :],
                in0=stps[:, 2, k, :], scalar1=bd4_s[:, k:k + 1], scalar2=None,
                op0=ALU.mult)
        t1 = vo("v", "tensor_tensor", [t1], out=sin_f[:, k, :], in0=sin_f[:, k, :],
                in1=stps[:, 1, k, :], op=ALU.add)
        t1 = vo("v", "tensor_scalar", [t1], out=sin_f[:, k, :], in0=sin_f[:, k, :],
                scalar1=bd4_s[:, k:k + 1], scalar2=None, op0=ALU.mult)
        t1 = vo("v", "tensor_tensor", [t1], out=sin_f[:, k, :], in0=sin_f[:, k, :],
                in1=stps[:, 0, k, :], op=ALU.add)
        c1 = vo("v", "tensor_copy", [t1, t_sz], out=sinb[0:64, k, 0:64],
                in_=sin_f[0:64, k, :])
        t_sinb = vo("v", "tensor_copy", [c1], out=sinb[64:128, k, 64:128],
                    in_=sin_f[64:128, k, :])

    t_c1 = P.barrier()
    qdc = sc[:, 0:1024]
    qf32 = sc[:, 1024:2048]
    pv = {}
    for kt in range(16):
        tqa = ac([t_c1, pv.get("qd0")], out=qd2[0:2, :], in_=ramp_s[0:2, :],
                 func=AF.Exp, scale=slpc_s[0:2, kt:kt + 1])
        tqc = []
        for ch in range(2):
            tm_ = mm(ps[4][:, 0:512], selq_s[:, :], qd2[0:2, 512 * ch:512 * ch + 512],
                     True, True, [tqa, tqc[-1] if tqc else pv.get("ps4")])
            tqc.append(vo("v", "tensor_copy", [tm_, pv.get("qdcr")],
                          out=qdc[:, 512 * ch:512 * ch + 512], in_=ps[4][:, 0:512]))
        pv["ps4"] = tqc[-1]
        pv["qd0"] = pv["qd1"] = tqc[-1]
        dq = kb.dma("y", qsb[:], qsili[:, kt, :], sem="qsd",
                    after=[t_c1, pv.get("qsbr")])
        t32 = vo("v", "tensor_copy", [dq, pv.get("qf32r")], out=qf32, in_=qsb[:])
        pv["qsbr"] = t32
        tmul = vo("v", "tensor_tensor", [t32] + tqc, out=qf32, in0=qf32, in1=qdc,
                  op=ALU.mult)
        pv["qdcr"] = tmul
        tqf = vo("v", "tensor_copy", [tmul] + pv.get("qfbr", []), out=qfb, in_=qf32)
        pv["qf32r"] = tqf
        rd = []
        for ch in range(2):
            tmm = mm(ps[ch][:, 0:512], sinb[:, kt, :], qfb[:, 512 * ch:512 * ch + 512],
                     True, True, [tqf, t_sinb, pv.get(("c1p", ch))])
            tadd = vo("v", "tensor_tensor", [tmm], out=XY[:, kt, 512 * ch:512 * ch + 512],
                      in0=XY[:, kt, 512 * ch:512 * ch + 512], in1=ps[ch][:, 0:512],
                      op=ALU.add)
            pv[("c1p", ch)] = tadd
            rd.append(tmm)
        pv["qfbr"] = rd

    t_c15 = P.barrier()
    sqs = sc[:, 2048:2560]
    toks_q = []
    for ch in range(2):
        for k in range(16):
            sq = ac([t_c15] + toks_q[-1:], out=sqs, in_=XY[:, k, 512 * ch:512 * ch + 512],
                    func=AF.Square)
            toks_q.append(mm(ps[3][0:1, :], onesc[:, 0:1], sqs, k == 0, k == 15,
                             [sq, t_oc] + toks_q[-1:]))
        toks_q.append(vo("v", "tensor_copy", [toks_q[-1]],
                         out=rows[0:1, 2048 + 512 * ch:2560 + 512 * ch],
                         in_=ps[3][0:1, :]))
    tv = vo("v", "tensor_scalar", [toks_q[-1]], out=rows[0:1, 2048:3072],
            in0=rows[0:1, 2048:3072], scalar1=1.0 / DM, scalar2=EPS_RMS1,
            op0=ALU.mult, op1=ALU.add)
    tsd = ac([tv], out=rows[0:1, 2048:3072], in_=rows[0:1, 2048:3072], func=AF.Sqrt)
    trc = vo("v", "reciprocal", [tsd], out=rows[0:1, 2048:3072],
             in_=rows[0:1, 2048:3072])
    reps = []
    for ch in range(2):
        t5 = mm(ps[2][:, :], ones1[0:1, :], rows[0:1, 2048 + 512 * ch:2560 + 512 * ch],
                True, True, [trc] + reps[-1:])
        reps.append(vo("v", "tensor_copy", [t5], out=rrep[:, 512 * ch:512 * ch + 512],
                       in_=ps[2][:, :]))
    t_rr = reps[-1]
    gs32 = sc[:, 0:1024]
    ynt = sc[:, 1024:2048]
    pg = {}
    for k in range(16):
        dg = kb.dma("g", gsb[:], gsigi[:, k, :], sem="gsd", after=[t_c15, pg.get("gsr")])
        t1 = vo("v", "tensor_tensor", [t_rr, pg.get("ynr")], out=ynt, in0=XY[:, k, :],
                in1=rrep, op=ALU.mult)
        t1 = vo("v", "tensor_scalar", [t1], out=ynt, in0=ynt,
                scalar1=normw_s[:, k:k + 1], scalar2=None, op0=ALU.mult)
        t3 = vo("g", "tensor_copy", [dg, pg.get("g32r")], out=gs32, in_=gsb[:])
        pg["gsr"] = t3
        t4 = vo("v", "tensor_tensor", [t1, t3], out=ynt, in0=ynt, in1=gs32, op=ALU.mult)
        pg["g32r"] = t4
        pg["ynr"] = vo("v", "tensor_copy", [t4], out=gyn[:, k, :], in_=ynt)

    t_c2 = P.barrier()
    wl = [None, None]
    pev = {}
    wob_prev = {0: [], 1: []}
    x3t = sc[:, 0:1024]
    for f in range(16):
        i2 = f % 2
        dl = kb.dma("y", wstr[i2].rearrange("p a b -> p (a b)"),
                    w1out[f].rearrange("p a b -> p (a b)"), sem=f"w{i2}d",
                    after=[t_c2, wl[i2]])
        dx3 = kb.dma("s", x3t, x3i[:, f, :], sem="x3d",
                     after=[t_c2, pev.get("x3r0"), pev.get("x3r1")])
        toks = {0: [], 1: []}
        for k in range(16):
            kp = k % 2
            cvp = vo("g", "tensor_copy", [dl] +
                     ([toks[0][-2], toks[1][-2]] if k >= 2
                      else [t_c2, wl[i2]] + wob_prev[kp]),
                     out=wob[:, kp, :], in_=wstr[i2][:, k, :])
            for ch in range(2):
                toks[ch].append(mm(ps[ch][:, :], wob[:, kp, :],
                                   gyn[:, k, 512 * ch:512 * ch + 512], k == 0, k == 15,
                                   [cvp] + ([pev[("c2p", ch)]] if ("c2p", ch) in pev else [])))
        wl[i2] = toks[1][-1]
        wob_prev[0] = [toks[0][14], toks[1][14]]
        wob_prev[1] = [toks[0][15], toks[1][15]]
        for ch in range(2):
            t1 = vo("v", "tensor_scalar", [toks[ch][-1], t_ga],
                    out=sc[:, 1024 + 512 * ch:1536 + 512 * ch], in0=ps[ch][:, :],
                    scalar1=gacl[:, f:f + 1], scalar2=None, op0=ALU.mult)
            pev[("c2p", ch)] = t1
            pev[f"x3r{ch}"] = vo("v", "tensor_tensor", [t1, dx3],
                                 out=XY[:, f, 512 * ch:512 * ch + 512],
                                 in0=x3t[:, 512 * ch:512 * ch + 512],
                                 in1=sc[:, 1024 + 512 * ch:1536 + 512 * ch], op=ALU.add)

    t_c3 = P.barrier()
    sqs3 = sc[:, 1024:1536]
    toks_s = []
    toks_q = []
    for ch in range(2):
        for k in range(16):
            toks_s.append(mm(ps[2][0:1, :], onesc[:, 0:1],
                             XY[:, k, 512 * ch:512 * ch + 512], k == 0, k == 15,
                             [t_c3, t_oc] + toks_s[-1:]))
            sq = ac([t_c3] + toks_q[-1:], out=sqs3,
                    in_=XY[:, k, 512 * ch:512 * ch + 512], func=AF.Square)
            toks_q.append(mm(ps[3][0:1, :], onesc[:, 0:1], sqs3, k == 0, k == 15,
                             [sq, t_oc] + toks_q[-1:]))
        toks_s.append(vo("v", "tensor_copy", [toks_s[-1]],
                         out=rows[0:1, 1024 + 512 * ch:1536 + 512 * ch],
                         in_=ps[2][0:1, :]))
        toks_q.append(vo("v", "tensor_copy", [toks_q[-1]],
                         out=rows[0:1, 2048 + 512 * ch:2560 + 512 * ch],
                         in_=ps[3][0:1, :]))
    tmn = vo("v", "tensor_scalar_mul", [toks_s[-1]], out=rows[0:1, 1024:2048],
             in0=rows[0:1, 1024:2048], scalar1=1.0 / DM)
    tv = vo("v", "tensor_scalar_mul", [toks_q[-1]], out=rows[0:1, 2048:3072],
            in0=rows[0:1, 2048:3072], scalar1=1.0 / DM)
    tms = vo("v", "tensor_tensor", [tmn], out=rows[0:1, 0:1024],
             in0=rows[0:1, 1024:2048], in1=rows[0:1, 1024:2048], op=ALU.mult)
    tv = vo("v", "tensor_tensor", [tv, tms], out=rows[0:1, 2048:3072],
            in0=rows[0:1, 2048:3072], in1=rows[0:1, 0:1024], op=ALU.subtract)
    tsd = ac([tv], out=rows[0:1, 2048:3072], in_=rows[0:1, 2048:3072], func=AF.Sqrt,
             bias=EPS_LN)
    trc = vo("v", "reciprocal", [tsd], out=rows[0:1, 2048:3072],
             in_=rows[0:1, 2048:3072])
    tmr_ = vo("v", "tensor_tensor", [trc], out=rows[0:1, 1024:2048],
              in0=rows[0:1, 1024:2048], in1=rows[0:1, 2048:3072], op=ALU.mult)
    outs = []
    for ch in range(2):
        t5 = mm(ps[2][:, :], ones1[0:1, :], rows[0:1, 2048 + 512 * ch:2560 + 512 * ch],
                True, True, [trc] + outs)
        outs.append(vo("v", "tensor_copy", [t5], out=rrep[:, 512 * ch:512 * ch + 512],
                       in_=ps[2][:, :]))
        t6 = mm(ps[3][:, :], ones1[0:1, :], rows[0:1, 1024 + 512 * ch:1536 + 512 * ch],
                True, True, [tmr_] + outs)
        outs.append(vo("v", "tensor_copy", [t6], out=mrep[:, 512 * ch:512 * ch + 512],
                       in_=ps[3][:, :]))
    t_ln4 = outs[-1]
    h4t = sc[:, 0:1024]
    h4toks = []
    for k in range(16):
        t1 = vo("v", "tensor_tensor", [t_ln4] + h4toks[-1:], out=h4t, in0=XY[:, k, :],
                in1=rrep, op=ALU.mult)
        t1 = vo("v", "tensor_tensor", [t1], out=h4t, in0=h4t, in1=mrep, op=ALU.subtract)
        t1 = vo("v", "tensor_scalar", [t1, t_cb], out=h4t, in0=h4t,
                scalar1=cbp1[:, k:k + 1], scalar2=modcC[:, 16 + k:17 + k],
                op0=ALU.mult, op1=ALU.add)
        h4toks.append(vo("v", "tensor_copy", [t1], out=h4b[:, k, :], in_=h4t))
    t_h4 = h4toks[-1]

    t_c4 = P.barrier()
    silt = sc[:, 0:512]
    hmc = [sc[:, 512:1024], sc[:, 1024:1536]]
    sqc = [sc[:, 1536:2048], sc[:, 2048:2560]]
    wlg = [None, None]
    wlu = [None, None]
    pev = {}
    statoks = {0: [], 1: []}
    for mp in range(64):
        i2 = mp % 2
        dg = kb.dma("y", WGB[:, i2].rearrange("p a b -> p (a b)"),
                    wgu[mp].rearrange("p a b -> p (a b)"), sem=f"w{i2}d",
                    after=[t_c4, wlg[i2]])
        du = kb.dma("s", WUB[:, i2].rearrange("p a b -> p (a b)"),
                    wgu[64 + mp].rearrange("p a b -> p (a b)"), sem=f"v{i2}d",
                    after=[t_c4, wlu[i2]])
        for ch in range(2):
            tg = []
            tu = []
            for k in range(16):
                tg.append(mm(ps[0][:, :], WGB[:, i2, k, :],
                             h4b[:, k, 512 * ch:512 * ch + 512], k == 0, k == 15,
                             [dg, t_h4] + ([pev["g"]] if "g" in pev else [])))
                tu.append(mm(ps[1][:, :], WUB[:, i2, k, :],
                             h4b[:, k, 512 * ch:512 * ch + 512], k == 0, k == 15,
                             [du] + ([pev["u"]] if "u" in pev else [])))
            tsil = ac([tg[-1], pev.get("mu")], out=silt, in_=ps[0][:, :], func=AF.Silu)
            pev["g"] = tsil
            tmu = vo("v", "tensor_tensor", [tu[-1], tsil] + pev.get(("hm", ch), []),
                     out=hmc[ch], in0=silt, in1=ps[1][:, :], op=ALU.mult)
            pev["u"] = tmu
            pev["mu"] = tmu
            dh = kb.dma("g", hm_d[mp][:, 512 * ch:512 * ch + 512], hmc[ch],
                        sem=f"hm{ch}d", after=[tmu])
            tsq = ac([tmu] + statoks[ch][-1:], out=sqc[ch], in_=hmc[ch], func=AF.Square)
            pev[("hm", ch)] = [tsq, dh]
            statoks[ch].append(mm(ps[2 + ch][0:1, :], onesc[:, 0:1], sqc[ch],
                                  mp == 0, mp == 63, [tsq, t_oc] + statoks[ch][-1:]))
        wlg[i2] = tg[-1]
        wlu[i2] = tu[-1]
    rq = []
    for ch in range(2):
        rq.append(vo("v", "tensor_copy", [statoks[ch][-1]],
                     out=rows[0:1, 2048 + 512 * ch:2560 + 512 * ch],
                     in_=ps[2 + ch][0:1, :]))
    tv = vo("v", "tensor_scalar", [rq[-1]] + rq[:1], out=rows[0:1, 2048:3072],
            in0=rows[0:1, 2048:3072], scalar1=1.0 / F0, scalar2=EPS_RMS2,
            op0=ALU.mult, op1=ALU.add)
    tsd = ac([tv], out=rows[0:1, 2048:3072], in_=rows[0:1, 2048:3072], func=AF.Sqrt)
    trc = vo("v", "reciprocal", [tsd], out=rows[0:1, 2048:3072],
             in_=rows[0:1, 2048:3072])
    reps = []
    for ch in range(2):
        t5 = mm(ps[2][:, :], ones1[0:1, :], rows[0:1, 2048 + 512 * ch:2560 + 512 * ch],
                True, True, [trc] + reps[-1:] + statoks[ch][-1:])
        reps.append(vo("v", "tensor_copy", [t5], out=rrep[:, 512 * ch:512 * ch + 512],
                       in_=ps[2][:, :]))
    t_rr2 = reps[-1]

    hld = [sc[:, 0:512], sc[:, 512:1024]]
    hsc = sc[:, 1024:1536]
    x5t = sc[:, 1536:2048]
    for th in range(2):
        t_th = P.barrier()
        packtoks = []
        pk = {}
        for k64 in range(64):
            i2 = k64 % 2
            dh = kb.dma("y", hld[i2], hm_d[k64][:, 512 * th:512 * th + 512],
                        sem=f"x{i2}d", after=[t_th, pk.get(("h", i2))])
            tm_ = vo("v", "tensor_tensor", [dh, t_rr2, pk.get("hscr")], out=hsc,
                     in0=hld[i2], in1=rrep[:, 512 * th:512 * th + 512], op=ALU.mult)
            pk[("h", i2)] = tm_
            tb = vo("v", "tensor_copy", [tm_], out=hnb[:, k64, :], in_=hsc)
            pk["hscr"] = tb
            packtoks.append(tb)
        t_pk = packtoks[-1]
        wlp = [None, None]
        pevp = {}
        for f in range(16):
            toks = []
            for kh in range(2):
                u = 2 * f + kh
                i2 = u % 2
                dw = kb.dma("y", WPB[:, i2].rearrange("p a b -> p (a b)"),
                            wpj[f, :, 32 * kh:32 * kh + 32, :].rearrange("p a b -> p (a b)"),
                            sem=f"w{i2}d", after=[t_th, wlp[i2]])
                for k in range(32):
                    toks.append(mm(ps[f % 2][:, :], WPB[:, i2, k, :], hnb[:, 32 * kh + k, :],
                                   kh == 0 and k == 0, kh == 1 and k == 31,
                                   [dw, t_pk] + ([pevp[("p", f % 2)]]
                                                 if (kh == 0 and k == 0 and ("p", f % 2) in pevp) else [])))
                wlp[i2] = toks[-1]
            t1 = vo("v", "tensor_scalar", [toks[-1], t_gb, pevp.get("x5r")], out=x5t,
                    in0=ps[f % 2][:, :], scalar1=gbcl[:, f:f + 1], scalar2=None,
                    op0=ALU.mult)
            pevp[("p", f % 2)] = t1
            t2 = vo("v", "tensor_tensor", [t1], out=x5t, in0=x5t,
                    in1=XY[:, f, 512 * th:512 * th + 512], op=ALU.add)
            do = kb.dma("g", outo[:, f, 512 * th:512 * th + 512], x5t, sem="outd",
                        after=[t2])
            pevp["x5r"] = do
    t_fin = P.barrier()
    kb.fence("y", [t_fin])
    P.close()
    return P


def a_inmaps(I, c):
    b, j = c // 4, c % 4
    t0 = TOK * j
    f32 = np.float32
    u = np.asarray(I["L0_attn_u"][b], f32)
    up = np.zeros((9 * 128, DM), f32)
    lo = max(t0 - HALO, 0)
    up[HALO - (t0 - lo):HALO + TOK] = u[lo:t0 + TOK]
    cswap = np.stack([I["L0_c"][b], I["L0_c"][1 - b],
                      I["L1_c"][b], I["L1_c"][1 - b]], 0).astype(f32)
    cvec = np.ascontiguousarray(cswap.reshape(4, 16, 128).transpose(2, 0, 1))
    fs = 1536 * c
    win = np.asarray(I["L0_attn_w_in"], f32)
    sel = np.zeros((32, 4096), f32)
    for jj in range(32):
        sel[jj, 128 * jj:128 * jj + 128] = 1.0
    wxbc = np.concatenate([I["L0_attn_w_xb"], I["L0_attn_w_xc"]], 0).astype(f32)
    wdt = np.asarray(I["L0_attn_w_dt"], f32)
    m3 = np.zeros((128, 3), f32) if j == 0 else np.ones((128, 3), f32)
    return {
        "u9": np.ascontiguousarray(up.reshape(9, 128, DM)),
        "cvec": cvec,
        "wm0": np.ascontiguousarray(np.asarray(I["L0_wmod"], f32)[:, fs:fs + 1536]
                                    .reshape(16, 128, 1536)),
        "wm1": np.ascontiguousarray(np.asarray(I["L1_wmod"], f32)[:, fs:fs + 1536]
                                    .reshape(16, 128, 1536)),
        "wmsa": np.ascontiguousarray(np.asarray(I["L0_wmod"], f32)[:, 0:4096]
                                     .reshape(16, 128, 4096)),
        "w_z": sb_tiles(np.ascontiguousarray(win[DI:].T)),
        "w_x": sb_tiles(np.ascontiguousarray(win[:DI].T)),
        "b_z": col_tiles(I["L0_attn_b_in"][DI:]),
        "b_x": col_tiles(I["L0_attn_b_in"][:DI]),
        "convw": ptile3(np.asarray(I["L0_attn_conv_w"], f32)[:, 0, :]),
        "convb": col_tiles(I["L0_attn_conv_b"]),
        "dpar": col_tiles(I["L0_attn_D_param"]),
        "alog": ptile3(np.asarray(I["L0_attn_A_log"], f32)),
        "bdt": col_tiles(I["L0_attn_b_dt"]),
        "wxdt": ptile3(np.ascontiguousarray(np.asarray(I["L0_attn_w_xdt"], f32).T)),
        "wxbc": ptile3(np.ascontiguousarray(wxbc.T)),
        "wdt": np.ascontiguousarray(wdt.reshape(32, 128, 128).transpose(2, 0, 1)),
        "mask3": m3,
        "idn": np.eye(128, dtype=f32),
        "selbc": sel,
        "w1gu_q": np.ascontiguousarray(np.asarray(I["L1_ffn_w_gate_up"], f32)[2048 * c:2048 * c + 2048].reshape(16, 128, DM)),
        "w1pj_q": np.ascontiguousarray(np.asarray(I["L1_ffn_w_proj"], f32)[256 * c:256 * c + 256].reshape(2, 128, F0)),
        "w0gu_a": np.ascontiguousarray(np.asarray(I["L0_ffn_w_gate_up"], f32)[2048 * c:2048 * c + 2048].reshape(16, 128, DM)),
        "w0dn_a": np.ascontiguousarray(np.asarray(I["L0_ffn_w_down"], f32)[256 * c:256 * c + 256].reshape(2, 128, F0)),
    }


def c_inmaps(I, c, relA, relB):
    b, j = c // 4, c % 4
    f32 = np.float32
    zpad = np.zeros((128, 16, HD), f32)
    stl = []
    for back in (1, 2, 3):
        jj = j - back
        stl.append(np.asarray(relB[4 * b + jj]["stoo"], f32) if jj >= 0 else zpad)
    modr1 = np.concatenate([relA[cc]["modsh"][1, 0 if cc // 4 == b else 1][None, :]
                            for cc in range(8)], axis=1).astype(f32)
    w1gu = np.concatenate([np.asarray(relA[cc]["w1gu_h"]).reshape(2048, 2048)
                           for cc in range(8)], axis=0)
    w1pj = np.concatenate([np.asarray(relA[cc]["w1pj_h"]).reshape(256, F0)
                           for cc in range(8)], axis=0)
    bd4c = np.empty((128, 16), f32)
    for k in range(16):
        bd4c[0:64, k] = BD4[2 * k]
        bd4c[64:128, k] = BD4[2 * k + 1]
    ramp2 = np.broadcast_to(np.arange(1, TOK + 1, dtype=f32)[None, :], (2, TOK)).copy()
    selq = np.zeros((2, 128), f32)
    selq[0, 0:64] = 1.0
    selq[1, 64:128] = 1.0
    return {
        "x3i": np.asarray(relB[c]["x3o"], f32),
        "ylai": np.asarray(relB[c]["ylao"], f32),
        "qsili": np.asarray(relB[c]["qsilo"]),
        "gsigi": np.asarray(relB[c]["gsigo"]),
        "stp": np.ascontiguousarray(np.stack(stl)),
        "modr1": np.ascontiguousarray(modr1),
        "normw": col_tiles(I["L1_attn_norm_w"]),
        "w1out": sb_tiles(np.ascontiguousarray(np.asarray(I["L1_attn_out_w"], f32).T)),
        "wgu": sb_tiles(np.ascontiguousarray(w1gu.T)),
        "wpj": sb_tiles(np.ascontiguousarray(w1pj.T)),
        "ramp2": ramp2,
        "bd4c": bd4c,
        "selq": selq,
        "slpc": np.array([[-SLOPES[2 * k + r] for k in range(16)] for r in range(2)],
                         f32),
        "idn": np.eye(128, dtype=f32),
    }


_A_KEYS = ("modsh", "zsil", "yloc", "cumdt", "cmt", "svec", "pvec", "w0part",
           "w1gu_h", "w1pj_h")
_B_KEYS = ("x3o", "ylao", "stoo", "gsigo", "qsilo")


def kernel(**inputs):
    I = {k: np.asarray(v) for k, v in inputs.items()}
    PA = build_A()
    mapsA = [a_inmaps(I, c) for c in range(8)]
    resA = run_bass_kernel_spmd(PA.nc, mapsA, core_ids=list(range(8)))
    relA = {c: {k: np.asarray(resA.results[c][k]) for k in _A_KEYS} for c in range(8)}
    PB = build_B()
    mapsB = [b_inmaps(I, c, relA) for c in range(8)]
    resB = run_bass_kernel_spmd(PB.nc, mapsB, core_ids=list(range(8)))
    relB = {c: {k: np.asarray(resB.results[c][k]) for k in _B_KEYS} for c in range(8)}
    PC = build_C()
    mapsC = [c_inmaps(I, c, relA, relB) for c in range(8)]
    resC = run_bass_kernel_spmd(PC.nc, mapsC, core_ids=list(range(8)))
    out = np.zeros((B, SEQ, DM), np.float32)
    for c in range(8):
        oo = np.asarray(resC.results[c]["outo"], np.float32)
        out[c // 4, TOK * (c % 4):TOK * (c % 4) + TOK] = \
            oo.transpose(2, 1, 0).reshape(TOK, DM)
    return out


def b_inmaps(I, c, relA):
    b, j = c // 4, c % 4
    t0 = TOK * j
    f32 = np.float32
    u = np.asarray(I["L0_attn_u"][b], f32)[t0:t0 + TOK]
    zpad = np.zeros((128, 32, 16), f32)
    svl, pvl = [], []
    for back in (1, 2, 3):
        jj = j - back
        if jj >= 0:
            svl.append(relA[4 * b + jj]["svec"])
            pvl.append(relA[4 * b + jj]["pvec"])
        else:
            svl.append(zpad)
            pvl.append(zpad)
    modr0 = np.concatenate([relA[cc]["modsh"][0, 0 if cc // 4 == b else 1][None, :]
                            for cc in range(8)], axis=1)
    modr1 = np.concatenate([relA[cc]["modsh"][1, 0 if cc // 4 == b else 1][None, :]
                            for cc in range(8)], axis=1)
    sel = np.zeros((32, 4096), f32)
    for jj in range(32):
        sel[jj, 128 * jj:128 * jj + 128] = 1.0
    r = np.arange(BS, dtype=f32) + 1.0
    dmTneg = np.where(r[:, None] >= r[None, :], -(r[:, None] - r[None, :]), f32(-1e30)).T
    dmt = np.ascontiguousarray(dmTneg.reshape(2, 128, 256).transpose(1, 0, 2))
    rk = (np.arange(BS, dtype=f32) + 1.0 - BS).reshape(2, 128).T
    return {
        "u8": np.ascontiguousarray(u.reshape(8, 128, DM)),
        "yloc": relA[c]["yloc"], "cumdt": relA[c]["cumdt"], "zsil": relA[c]["zsil"],
        "cmt": relA[c]["cmt"],
        "svp": np.ascontiguousarray(np.stack(svl)), "pvp": np.ascontiguousarray(np.stack(pvl)),
        "alog": ptile3(np.asarray(I["L0_attn_A_log"], f32)),
        "modr0": np.ascontiguousarray(modr0), "modr1": np.ascontiguousarray(modr1),
        "w0pg": np.concatenate([relA[cc]["w0part"][:, 0:1] for cc in range(8)], axis=1),
        "w0pd": np.concatenate([relA[cc]["w0part"][:, 1:2] for cc in range(8)], axis=1),
        "wout": sb_tiles(np.ascontiguousarray(np.asarray(I["L0_attn_w_out"], f32).T)),
        "bout": col_tiles(I["L0_attn_b_out"]),
        "w0gu": sb_tiles(np.ascontiguousarray(np.asarray(I["L0_ffn_w_gate_up"], f32).T)),
        "w0dn": sb_tiles(np.ascontiguousarray(np.asarray(I["L0_ffn_w_down"], f32).T)),
        "wqkv": sb_tiles(np.ascontiguousarray(np.asarray(I["L1_attn_qkv_w"], f32).T)),
        "wgate": sb_tiles(np.ascontiguousarray(np.asarray(I["L1_attn_gate_w"], f32).T)),
        "idn": np.eye(128, dtype=f32),
        "dmt": np.ascontiguousarray(dmt.reshape(128, 512)),
        "rampr": r[None, :].copy(),
        "rkdc": np.ascontiguousarray(rk),
    }
